# version 14
# baseline (speedup 1.0000x reference)
import contextlib
import numpy as np
import concourse.bass as bass
import concourse.mybir as mybir

F32 = mybir.dt.float32
BF16 = mybir.dt.bfloat16
AL = mybir.AluOpType
ACT = mybir.ActivationFunctionType

PP = 128
AG_ON_SYNC = True


class Cfg:
    def __init__(self, D=2048, L=4096, INTER=5632, SPA=512, SPB=1024, NST=16,
                 n_cores=8):
        self.D, self.L, self.INTER, self.NST = D, L, INTER, NST
        self.SPA, self.SPB = SPA, SPB
        self.n_cores = n_cores
        self.DH = D // 2
        self.NSP = L // SPA
        self.ndc = D // PP
        self.nhc = self.DH // PP
        self.nic = INTER // PP
        self.nih = self.nic // 2
        self.nmc = self.ndc
        self.cwA = min(512, SPA)
        self.ncgA = SPA // self.cwA
        self.cwB = min(512, SPB)
        self.ncgB = SPB // self.cwB
        self.GPS = SPB // SPA
        self.nown = L // (2 * SPB)
        assert L % SPA == 0 and self.NSP % 2 == 0 and SPB % SPA == 0
        assert D % 256 == 0 and INTER % (2 * PP) == 0 and L % (2 * SPB) == 0
        self.eps = 1e-5


def build(cfg: Cfg):
    nc = bass.Bass("TRN2", target_bir_lowering=False)
    c = cfg
    SPA, SPB, NST = c.SPA, c.SPB, c.NST

    x_fm = nc.dram_tensor("x_fm", [c.D, c.L], F32, kind="ExternalInput").ap()
    x_h = nc.dram_tensor("x_h", [c.DH, c.L], F32, kind="ExternalInput").ap()
    xB = nc.dram_tensor("xB", [c.D, c.nown * SPB], F32, kind="ExternalInput").ap()
    wdt = nc.dram_tensor("wdt", [c.nhc, PP, c.ndc * PP], F32, kind="ExternalInput").ap()
    wxp = nc.dram_tensor("wxp", [PP, c.ndc * 2 * NST], F32, kind="ExternalInput").ap()
    wout = nc.dram_tensor("wout", [c.nmc, PP, c.ndc * PP], F32, kind="ExternalInput").ap()
    wg = nc.dram_tensor("wg", [c.nih, PP, c.ndc * PP], F32, kind="ExternalInput").ap()
    wu = nc.dram_tensor("wu", [c.nih, PP, c.ndc * PP], F32, kind="ExternalInput").ap()
    wd = nc.dram_tensor("wd", [c.nmc, PP, c.nic * PP], F32, kind="ExternalInput").ap()
    alog = nc.dram_tensor("alog", [PP, c.nhc * NST], F32, kind="ExternalInput").ap()
    dtb = nc.dram_tensor("dtb", [PP, c.nhc], F32, kind="ExternalInput").ap()
    xpb = nc.dram_tensor("xpb", [2 * NST, 1], F32, kind="ExternalInput").ap()
    outb = nc.dram_tensor("outb", [PP, c.nmc], F32, kind="ExternalInput").ap()
    dsc = nc.dram_tensor("dsc", [PP, c.nmc], F32, kind="ExternalInput").ap()
    gsc = nc.dram_tensor("gsc", [PP, c.nih], F32, kind="ExternalInput").ap()
    usc = nc.dram_tensor("usc", [PP, c.nih], F32, kind="ExternalInput").ap()
    mks = nc.dram_tensor("mks", [PP, 2], F32, kind="ExternalInput").ap()
    out = nc.dram_tensor("out", [c.D, c.nown * SPB], F32, kind="ExternalOutput").ap()
    cc_in = [nc.dram_tensor(f"cc_in{k}", [c.DH, SPA], BF16, kind="Internal").ap()
             for k in range(c.NSP)]
    cc_out = [nc.dram_tensor(f"cc_out{k}", [c.D, SPA], BF16, kind="Internal").ap()
              for k in range(c.NSP)]
    wdtb = nc.dram_tensor("wdtb", [c.nhc, PP, c.ndc * PP], BF16, kind="Internal").ap()
    wub = nc.dram_tensor("wub", [c.nih, PP, c.ndc * PP], BF16, kind="Internal").ap()
    woutb = nc.dram_tensor("woutb", [c.nmc, PP, c.ndc * PP], BF16, kind="Internal").ap()
    wdb = nc.dram_tensor("wdb", [c.nmc, PP, c.nic * PP], BF16, kind="Internal").ap()
    wgb = nc.dram_tensor(
        "wgb", [max(1, c.nih - max(1, (c.nih + 1) // 2 - 5)), PP, c.ndc * PP],
        BF16, kind="Internal").ap()
    cc_hin = [nc.dram_tensor(f"cc_hin{k}", [c.nih * PP, SPA], BF16, kind="Internal").ap()
              for k in range(c.NSP)]
    cc_hout = [nc.dram_tensor(f"cc_hout{k}", [c.nic * PP, SPA], BF16, kind="Internal").ap()
               for k in range(c.NSP)]
    groups = [[2 * p, 2 * p + 1] for p in range(max(1, c.n_cores // 2))]

    es = contextlib.ExitStack()
    with es:
        sb = lambda name, shape, dt: es.enter_context(nc.sbuf_tensor(name, shape, dt))
        sA = sb("sA", [PP, c.nhc * NST], F32)
        sK = sb("sK", [PP, 3], F32)
        sdtb = sb("sdtb", [PP, c.nhc], F32)
        sxpb = sb("sxpb", [2 * NST, 1], F32)
        soutb = sb("soutb", [PP, c.nmc], F32)
        sds = sb("sds", [PP, c.nmc], F32)
        sgs = sb("sgs", [PP, c.nih], F32)
        sus = sb("sus", [PP, c.nih], F32)
        smks = sb("smks", [PP, 2], F32)
        sWxp = sb("sWxp", [PP, c.ndc * 2 * NST], BF16)
        sones_c = sb("sones_c", [PP, 1], F32)
        sones_r = sb("sones_r", [96, PP], F32)
        sH = sb("sH", [PP, c.nhc * NST], F32)

        class Ch:
            def __init__(self, name):
                self.sem = es.enter_context(nc.semaphore(name))
                self.n = 0
            def inc(self, instr, k=1):
                instr.then_inc(self.sem, k)
                self.n += k
                return instr
        chs = {}
        def ch(name):
            if name not in chs:
                chs[name] = Ch(name)
            return chs[name]

        block = es.enter_context(nc.Block())
        ENG = ["tensor", "vector", "scalar", "gpsimd", "sync"]

        def emit(eng, fn):
            getattr(block, eng)(fn)

        bsem = ch("bsem")
        def barrier():
            bsem.n += 5
            target = bsem.n
            for e in ENG:
                def f(eng, _t=target):
                    eng.drain().then_inc(bsem.sem, 1)
                    eng.wait_ge(bsem.sem, _t)
                emit(e, f)

        def load_act(eng, idx):
            eng.add_instruction(mybir.InstLoadActFuncSet(
                name=nc.get_next_instruction_name(), act_func_set_id=idx,
                ins=[], outs=[]))

        ld_c = ch("ld_c")
        ld_g = ch("ld_g")
        ccast_dt = ch("ccast_dt")
        def pre_sync(eng):
            for dst, src in [(sA[:], alog), (sdtb[:], dtb), (sxpb[:], xpb),
                             (soutb[:], outb), (sds[:], dsc), (sgs[:], gsc),
                             (sus[:], usc), (smks[:], mks)]:
                ld_c.inc(eng.dma_start(out=dst, in_=src), 16)
        emit("sync", pre_sync)
        def pre_gp(eng):
            ld_g.inc(eng.dma_start(out=sWxp[:], in_=wxp), 16)
            for ec in range(c.nhc):
                ccast_dt.inc(eng.dma_start(out=wdtb[ec], in_=wdt[ec]), 16)
        emit("gpsimd", pre_gp)
        def pre_v(eng):
            eng.memset(sones_c[:], 1.0)
            eng.memset(sones_r[:], 1.0)
            eng.memset(sH[:], 0.0)
            eng.memset(sK[:, 0:1], float(c.eps))
            eng.memset(sK[:, 1:2], 1.0)
            eng.memset(sK[:, 2:3], -0.5)
        emit("vector", pre_v)
        sA_s = ch("sA_s")
        def pre_s(eng):
            load_act(eng, 6)
            eng.wait_ge(ld_c.sem, ld_c.n)
            sA_s.inc(eng.activation(sA[:], sA[:], ACT.Exp))
        emit("scalar", pre_s)
        def pre_v2(eng):
            eng.wait_ge(sA_s.sem, sA_s.n)
            eng.tensor_scalar_mul(sA[:], sA[:], -1.0)
        emit("vector", pre_v2)
        barrier()

        cag = ch("cag")

        cw, ncg = c.cwA, c.ncgA
        NIT = NST * c.nhc
        with contextlib.ExitStack() as esA:
            sbA = lambda name, shape, dt: esA.enter_context(nc.sbuf_tensor(name, shape, dt))
            sxc = [sbA(f"sxc{i}", [PP, SPA], F32) for i in range(2)]
            sxn = [sbA(f"sxn{i}", [PP, SPA], F32) for i in range(4)]
            st1 = [sbA(f"st1_{i}", [PP, SPA], F32) for i in range(2)]
            stmp = sbA("stmp", [PP, SPA], F32)
            srs_b = sbA("srs_b", [PP, SPA], F32)
            smrs_b = sbA("smrs_b", [PP, SPA], F32)
            sxnbfA = sbA("sxnbfA", [PP, c.ndc * SPA], BF16)
            sdt = sbA("sdt", [PP, c.nhc * SPA], F32)
            su = sbA("su", [PP, c.nhc * SPA], F32)
            sy = sbA("sy", [PP, c.nhc * SPA], F32)
            sWdt = [sbA(f"sWdt{i}", [PP, c.ndc * PP], BF16) for i in range(2)]
            sbc2 = sbA("sbc2", [96, 5 * SPA], F32)
            srs = sbc2[0:1, SPA:2 * SPA]
            smrs = sbc2[0:1, 2 * SPA:3 * SPA]
            sm_row = sbc2[0:1, 3 * SPA:4 * SPA]
            svar = sbc2[0:1, 4 * SPA:5 * SPA]
            sBb = [sbA(f"sBb{i}", [PP, SPA], F32) for i in range(2)]
            sCb = [sbA(f"sCb{i}", [PP, SPA], F32) for i in range(2)]
            sdA = [sbA(f"sdA{i}", [PP, SPA], F32) for i in range(3)]
            sdBx = [sbA(f"sdBx{i}", [PP, SPA], F32) for i in range(2)]
            sh_ = [sbA(f"sh{i}", [PP, SPA], F32) for i in range(2)]
            shCr = sbA("shCr", [PP, c.nhc * SPA], F32)
            NR = max(1, (c.nih + 1) // 2 - 5)
            sWgR = sbA("sWgR", [PP, NR * c.ndc * PP], BF16)
            sWgFs = [sbA(f"sWgFs{i}", [PP, c.ndc * PP], BF16) for i in range(2)]
            sWuF = [sbA(f"sWuF{i}", [PP, c.ndc * PP], BF16) for i in range(2)]
            ssgF = [sbA(f"ssgF{i}", [PP, SPA], F32) for i in range(4)]
            suev = [sbA(f"suev{i}", [PP, SPA], F32) for i in range(4)]
            shdnF = [sbA(f"shdnF{i}", [PP, SPA], BF16) for i in range(2)]
            ODW = 384
            sow = [sbA(f"sow{i}", [PP, ODW], F32) for i in range(2)]
            sob = [sbA(f"sob{i}", [PP, ODW], BF16) for i in range(2)]

            psA = contextlib.ExitStack()
            with psA:
                ps0 = psA.enter_context(nc.psum_tensor("ps0", [1, SPA], F32))
                ps1 = psA.enter_context(nc.psum_tensor("ps1", [1, SPA], F32))
                prep = psA.enter_context(nc.psum_tensor("prep", [PP, SPA], F32))
                pdt = psA.enter_context(nc.psum_tensor("pdt", [PP, SPA], F32))
                pB = psA.enter_context(nc.psum_tensor("pB", [PP, SPA], F32))
                pC = psA.enter_context(nc.psum_tensor("pC", [PP, SPA], F32))
                pgF = psA.enter_context(nc.psum_tensor("pgF", [PP, SPA], F32))
                puF = psA.enter_context(nc.psum_tensor("puF", [PP, SPA], F32))

                ldx = [ch("ldx0"), ch("ldx1")]
                ldn = [ch("ldn0"), ch("ldn1")]
                xmm, xsq, sqmm = ch("xmm"), ch("xsq"), ch("sqmm")
                rv, rs_ = ch("rowV"), ch("rowS")
                bb, be = ch("bb"), ch("be")
                vn, cu = ch("vnorm"), ch("cu")
                cwdt = [ch("cwdt0"), ch("cwdt1")]
                cpdt, cdtS = ch("cpdt"), ch("cdtS")
                cxp, cxpS = ch("cxp"), ch("cxpS")
                crow = [ch("crow0"), ch("crow1")]
                cbb, cbe = ch("cbb"), ch("cbe")
                cdA, cdAf = ch("cdA"), ch("cdAf")
                cadd = ch("cadd")
                chC, chCf = ch("chC"), ch("chCf")
                cdbx = ch("cdbx")
                cgur = ch("cgur")
                cwgF = [ch("cwgF0"), ch("cwgF1")]
                cwuF = [ch("cwuF0"), ch("cwuF1")]
                cpgF, cpuF = ch("cpgF"), ch("cpuF")
                csgF, cuev = ch("csgF"), ch("cuev")
                chdnF = ch("chdnF")
                chd = [ch("chd0"), ch("chd1")]
                cyd = ch("cyd")
                codl = [ch("codl0"), ch("codl1")]
                codc = ch("codc")
                cods = [ch("cods0"), ch("cods1")]
                NJG = max(1, c.nhc // 2)
                NADD = c.nhc // NJG

                nspan = max(1, min(NST - 4, 11))
                sil_at = {}
                for ic in range(c.nih):
                    nslot = 1 + (ic * nspan) // c.nih
                    sil_at.setdefault(nslot, []).append(ic)
                sil_gp = {n + 1: ics for n, ics in sil_at.items()}

                def chopA(total):
                    o, r = [], 0
                    while r < total:
                        o.append((r, min(r + ODW, total)))
                        r += ODW
                    return o
                od_pieces = []
                for mc in range(c.nmc):
                    for lo, hi in chopA(c.ndc * PP):
                        od_pieces.append((wout[mc], woutb[mc], lo, hi))
                    for lo, hi in chopA(c.nic * PP):
                        od_pieces.append((wd[mc], wdb[mc], lo, hi))
                NOD = len(od_pieces)
                od_cm = {}
                od_odm = {}
                for k2 in range(c.NSP):
                    p0, p1 = k2 * NOD // c.NSP, (k2 + 1) * NOD // c.NSP
                    m = {}
                    for idx in range(p1 - p0):
                        slot = min(NST - 1,
                                   1 + idx * (NST - 1) // max(1, p1 - p0))
                        m.setdefault(slot, []).append(p0 + idx)
                        od_cm[p0 + idx] = k2 * NST + slot
                    od_odm[k2] = m
                od_issue = {}
                for p in range(NOD):
                    gs = max(0, od_cm[p] - 2,
                             od_cm.get(p - 2, 0))
                    od_issue.setdefault(gs, []).append(p)


                ccast_u = ch("ccast_u")
                ccast_g = ch("ccast_g")
                def fgpPreA(eng):
                    for ic in range(c.nih):
                        if ic < NR:
                            cgur.inc(eng.dma_start(
                                out=sWgR[:, ic * c.ndc * PP:(ic + 1) * c.ndc * PP],
                                in_=wg[ic]), 16)
                        else:
                            ccast_g.inc(eng.dma_start(out=wgb[ic - NR],
                                                      in_=wg[ic]), 16)
                        ccast_u.inc(eng.dma_start(out=wub[ic], in_=wu[ic]), 16)
                emit("gpsimd", fgpPreA)

                tot_cyd, tot_chd0, tot_chd1 = [], [], []
                for k in range(c.NSP):
                    col0 = k * SPA
                    EC0 = k * c.nhc
                    GN0 = k * NST
                    GI0 = k * NIT
                    GIC0 = k * c.nih

                    b_ldS = [ldn[0].n, ldn[1].n]
                    b_xmm, b_xsq, b_sqmm = xmm.n, xsq.n, sqmm.n
                    def fsyncS(eng, k=k, col0=col0, b_xmm=b_xmm,
                               b_sqmm=b_sqmm, b_ldS=b_ldS):
                        for dc in range(c.ndc):
                            s = dc % 2
                            if dc < 4:
                                eng.wait_ge(vn.sem, vn.n)
                            else:
                                eng.wait_ge(xmm.sem, b_xmm + ncg * (dc - 3))
                                eng.wait_ge(sqmm.sem, b_sqmm + ncg * (dc - 3))
                            if dc >= 2:
                                eng.wait_ge(ldn[s].sem, b_ldS[s] + 16 * (dc // 2))
                            else:
                                eng.wait_ge(ldn[s].sem, b_ldS[s])
                            ldn[s].inc(eng.dma_start(
                                out=sxn[dc % 4][:],
                                in_=x_fm[dc * PP:(dc + 1) * PP, col0:col0 + SPA]), 16)
                    emit("sync", fsyncS)

                    def fTS(eng, k=k, b_ldS=b_ldS, b_xsq=b_xsq):
                        eng.wait_ge(rv.sem, rv.n)
                        for dc in range(c.ndc):
                            s = dc % 2
                            eng.wait_ge(ldn[s].sem, b_ldS[s] + 16 * (dc // 2 + 1))
                            for g in range(ncg):
                                xmm.inc(eng.matmul(ps0[:, g * cw:(g + 1) * cw],
                                                   sones_c[:],
                                                   sxn[dc % 4][:, g * cw:(g + 1) * cw],
                                                   start=(dc == 0), stop=(dc == c.ndc - 1)))
                            eng.wait_ge(xsq.sem, b_xsq + dc + 1)
                            for g in range(ncg):
                                sqmm.inc(eng.matmul(ps1[:, g * cw:(g + 1) * cw],
                                                    sones_c[:],
                                                    st1[s][:, g * cw:(g + 1) * cw],
                                                    start=(dc == 0), stop=(dc == c.ndc - 1)))
                    emit("tensor", fTS)

                    def fSS(eng, k=k, b_ldS=b_ldS, b_sqmm=b_sqmm):
                        eng.wait_ge(cu.sem, cu.n)
                        for dc in range(c.ndc):
                            s = dc % 2
                            eng.wait_ge(ldn[s].sem, b_ldS[s] + 16 * (dc // 2 + 1))
                            if dc >= 2:
                                eng.wait_ge(sqmm.sem, b_sqmm + ncg * (dc - 1))
                            xsq.inc(eng.activation(st1[s][:], sxn[dc % 4][:],
                                                   ACT.Square))
                    emit("scalar", fSS)

                    b_rv, b_rs, b_bb, b_be = rv.n, rs_.n, bb.n, be.n
                    def frowV(eng, b_rs=b_rs):
                        eng.wait_ge(xmm.sem, xmm.n)
                        eng.wait_ge(sqmm.sem, sqmm.n)
                        eng.tensor_scalar_mul(sm_row, ps0[0:1, :], 1.0 / c.D)
                        eng.tensor_scalar_mul(svar, ps1[0:1, :], 1.0 / c.D)
                        eng.drain()
                        eng.tensor_tensor(stmp[0:1, :], sm_row,
                                          sm_row, AL.mult)
                        eng.drain()
                        rv.inc(eng.tensor_tensor(svar, svar,
                                                 stmp[0:1, :], AL.subtract))
                        eng.wait_ge(rs_.sem, b_rs + 1)
                        rv.inc(eng.tensor_tensor(smrs, sm_row,
                                                 srs, AL.mult))
                    emit("vector", frowV)

                    def frowS(eng, b_rv=b_rv, b_bb=b_bb):
                        eng.wait_ge(rv.sem, b_rv + 1)
                        eng.activation(svar, svar, ACT.Ln,
                                       bias=sK[0:1, 0:1])
                        eng.drain()
                        rs_.inc(eng.activation(srs, svar,
                                               ACT.Exp, scale=sK[0:1, 2:3]))
                        eng.wait_ge(bb.sem, b_bb + 1)
                        be.inc(eng.activation(srs_b[:], prep[:], ACT.Copy))
                        eng.wait_ge(bb.sem, b_bb + 2)
                        be.inc(eng.activation(smrs_b[:], prep[:], ACT.Copy))
                    emit("scalar", frowS)

                    def fbT(eng, b_rv=b_rv, b_be=b_be):
                        eng.wait_ge(rv.sem, b_rv + 2)
                        eng.wait_ge(cu.sem, cu.n)
                        eng.wait_ge(cxpS.sem, cxpS.n)
                        last = None
                        for g in range(ncg):
                            last = eng.matmul(prep[:, g * cw:(g + 1) * cw],
                                              sones_r[0:1, :],
                                              sbc2[0:1, SPA + g * cw:
                                                   SPA + (g + 1) * cw],
                                              start=True, stop=True)
                        bb.inc(last)
                        eng.wait_ge(be.sem, b_be + 1)
                        last = None
                        for g in range(ncg):
                            last = eng.matmul(prep[:, g * cw:(g + 1) * cw],
                                              sones_r[0:1, :],
                                              sbc2[0:1, 2 * SPA + g * cw:
                                                   2 * SPA + (g + 1) * cw],
                                              start=True, stop=True)
                        bb.inc(last)
                    emit("tensor", fbT)

                    b_vn = vn.n
                    b_ldn = [ldn[0].n, ldn[1].n]
                    def fsyncN(eng, k=k, col0=col0, b_vn=b_vn, b_ldn=b_ldn,
                               b_xmm=b_xmm, b_sqmm=b_sqmm, b_xsq=b_xsq):
                        for dc in range(c.ndc):
                            s = dc % 2
                            if dc < 4:
                                t = dc + 4 * ((c.ndc - 1 - dc) // 4)
                                eng.wait_ge(xmm.sem, b_xmm + ncg * (t + 1))
                                eng.wait_ge(sqmm.sem, b_sqmm + ncg * (t + 1))
                                eng.wait_ge(xsq.sem, b_xsq + t + 1)
                            else:
                                eng.wait_ge(vn.sem, b_vn + dc - 3)
                            if dc >= 2:
                                eng.wait_ge(ldn[s].sem, b_ldn[s] + 16 * (dc // 2))
                            else:
                                eng.wait_ge(ldn[s].sem, b_ldn[s])
                            ldn[s].inc(eng.dma_start(
                                out=sxn[dc % 4][:],
                                in_=x_fm[dc * PP:(dc + 1) * PP, col0:col0 + SPA]), 16)
                    emit("sync", fsyncN)

                    def fVN(eng, k=k, b_ldn=b_ldn, b_be=b_be):
                        eng.wait_ge(be.sem, b_be + 2)
                        eng.wait_ge(cpuF.sem, cpuF.n)
                        for dc in range(c.ndc):
                            s = dc % 2
                            eng.wait_ge(ldn[s].sem, b_ldn[s] + 16 * (dc // 2 + 1))
                            eng.tensor_tensor(st1[s][:], sxn[dc % 4][:],
                                              srs_b[:], AL.mult)
                            eng.drain()
                            vn.inc(eng.tensor_tensor(
                                sxnbfA[:, dc * SPA:(dc + 1) * SPA], st1[s][:],
                                smrs_b[:], AL.subtract))
                    emit("vector", fVN)

                    b_cwdt = [cwdt[0].n, cwdt[1].n]
                    def fsyncW(eng, k=k, EC0=EC0):
                        for ec in range(c.nhc):
                            s = ec % 2
                            EC = EC0 + ec
                            eng.wait_ge(ccast_dt.sem, 16 * c.nhc)
                            if EC >= 2:
                                eng.wait_ge(cpdt.sem, EC - 1)
                            cwdt[s].inc(eng.dma_start(out=sWdt[s][:], in_=wdtb[ec]), 16)
                    emit("sync", fsyncW)

                    def fTdt(eng, k=k, EC0=EC0, b_cwdt=b_cwdt):
                        for ec in range(c.nhc):
                            s = ec % 2
                            EC = EC0 + ec
                            eng.wait_ge(cwdt[s].sem, b_cwdt[s] + 16 * (ec // 2 + 1))
                            eng.wait_ge(cdtS.sem, EC)
                            if ec == 0:
                                eng.wait_ge(vn.sem, vn.n)
                            last = None
                            for dc in range(c.ndc):
                                for g in range(ncg):
                                    last = eng.matmul(
                                        pdt[:, g * cw:(g + 1) * cw],
                                        sWdt[s][:, dc * PP:(dc + 1) * PP],
                                        sxnbfA[:, dc * SPA + g * cw: dc * SPA + (g + 1) * cw],
                                        start=(dc == 0), stop=(dc == c.ndc - 1))
                            cpdt.inc(last)
                        eng.wait_ge(ld_g.sem, 16)
                        eng.wait_ge(be.sem, be.n)
                        last = None
                        for dc in range(c.ndc):
                            for g in range(ncg):
                                last = eng.matmul(
                                    prep[0:2 * NST, g * cw:(g + 1) * cw],
                                    sWxp[:, dc * 2 * NST:(dc + 1) * 2 * NST],
                                    sxnbfA[:, dc * SPA + g * cw: dc * SPA + (g + 1) * cw],
                                    start=(dc == 0), stop=(dc == c.ndc - 1))
                        cxp.inc(last)
                    emit("tensor", fTdt)

                    def fSdt(eng, k=k, EC0=EC0):
                        for ec in range(c.nhc):
                            EC = EC0 + ec
                            eng.wait_ge(cpdt.sem, EC + 1)
                            eng.activation(stmp[:], pdt[:], ACT.Exp,
                                           bias=sdtb[:, ec:ec + 1])
                            eng.drain()
                            cdtS.inc(eng.activation(sdt[:, ec * SPA:(ec + 1) * SPA],
                                                    stmp[:], ACT.Ln, bias=sK[:, 1:2]))
                            eng.drain()
                    emit("scalar", fSdt)

                    def fVxp(eng, k=k):
                        eng.wait_ge(cxp.sem, k + 1)
                        eng.wait_ge(crow[0].sem, crow[0].n)
                        eng.wait_ge(crow[1].sem, crow[1].n)
                        cxpS.inc(eng.tensor_scalar(sbc2[0:2 * NST, 0:SPA],
                                                   prep[0:2 * NST, :],
                                                   sxpb[:, 0:1], None, AL.add))
                    emit("vector", fVxp)

                    b_ldxu = [ldx[0].n, ldx[1].n]
                    def fsyncU(eng, k=k, col0=col0):
                        for j in range(c.nhc):
                            s = j % 2
                            GJ = k * c.nhc + j
                            if j < 2:
                                eng.wait_ge(cu.sem, cu.n)
                                eng.wait_ge(ldx[s].sem, ldx[s].n)
                            else:
                                eng.wait_ge(cu.sem, GJ - 1)
                            ldx[s].inc(eng.dma_start(
                                out=sxc[s][:],
                                in_=x_h[j * PP:(j + 1) * PP, col0:col0 + SPA]), 16)
                    emit("sync", fsyncU)

                    def fVU(eng, k=k, EC0=EC0, b_ldxu=b_ldxu):
                        for j in range(c.nhc):
                            s = j % 2
                            eng.wait_ge(ldx[s].sem, b_ldxu[s] + 16 * (j // 2 + 1))
                            eng.wait_ge(cdtS.sem, EC0 + j + 1)
                            eng.tensor_tensor(st1[s][:], sxc[s][:],
                                              srs_b[:], AL.mult)
                            eng.drain()
                            eng.tensor_tensor(sxc[s][:], st1[s][:],
                                              smrs_b[:], AL.subtract)
                            eng.drain()
                            cu.inc(eng.tensor_tensor(su[:, j * SPA:(j + 1) * SPA],
                                                     sdt[:, j * SPA:(j + 1) * SPA],
                                                     sxc[s][:], AL.mult))
                    emit("vector", fVU)

                    b_crow = [crow[0].n, crow[1].n]
                    b_cwgF = [cwgF[0].n, cwgF[1].n]
                    def fsyncR(eng, k=k, GN0=GN0):
                        eng.wait_ge(cxpS.sem, k + 1)
                        for n in range(NST):
                            s = n % 2
                            GN = GN0 + n
                            if GN >= 2:
                                eng.wait_ge(cbb.sem, GN - 1)
                            p0 = 32 + 32 * s
                            crow[s].inc(eng.dma_start(
                                out=sbc2[p0:p0 + 1, 0:SPA],
                                in_=sbc2[n:n + 1, 0:SPA]), 16)
                            crow[s].inc(eng.dma_start(
                                out=sbc2[p0:p0 + 1, SPA:2 * SPA],
                                in_=sbc2[NST + n:NST + n + 1, 0:SPA]), 16)
                    emit("sync", fsyncR)

                    b_cwuF = [cwuF[0].n, cwuF[1].n]
                    def fTscan(eng, k=k, GN0=GN0, GIC0=GIC0,
                               b_crow=b_crow, b_cwuF=b_cwuF, b_cwgF=b_cwgF):
                        for n in range(NST):
                            s = n % 2
                            GN = GN0 + n
                            eng.wait_ge(crow[s].sem, b_crow[s] + 32 * (n // 2 + 1))
                            if GN >= 1:
                                eng.wait_ge(cbe.sem, GN)
                            p0 = 32 + 32 * s
                            for g in range(ncg):
                                eng.matmul(pB[:, g * cw:(g + 1) * cw],
                                           sones_r[p0:p0 + 1, :],
                                           sbc2[p0:p0 + 1, g * cw:(g + 1) * cw],
                                           start=True, stop=True)
                            last = None
                            for g in range(ncg):
                                last = eng.matmul(
                                    pC[:, g * cw:(g + 1) * cw],
                                    sones_r[p0:p0 + 1, :],
                                    sbc2[p0:p0 + 1, SPA + g * cw:SPA + (g + 1) * cw],
                                    start=True, stop=True)
                            cbb.inc(last)
                            for ic in (sil_at.get(n, []) if k >= 1 else []):
                                GIC = GIC0 + ic
                                if ic < NR:
                                    eng.wait_ge(cgur.sem, 16 * NR)
                                    gsrc = lambda dc, ic=ic: sWgR[
                                        :, (ic * c.ndc + dc) * PP:
                                        (ic * c.ndc + dc + 1) * PP]
                                else:
                                    si = ic - NR
                                    eng.wait_ge(cwgF[si % 2].sem,
                                                b_cwgF[si % 2] + 16 * (si // 2 + 1))
                                    gsrc = lambda dc, ic=ic: sWgFs[
                                        (ic - NR) % 2][:, dc * PP:(dc + 1) * PP]
                                eng.wait_ge(csgF.sem, GIC)
                                last = None
                                for dc in range(c.ndc):
                                    last = eng.matmul(
                                        pgF[:], gsrc(dc),
                                        sxnbfA[:, dc * SPA:(dc + 1) * SPA],
                                        start=(dc == 0), stop=(dc == c.ndc - 1))
                                cpgF.inc(last)
                                eng.wait_ge(cwuF[ic % 2].sem,
                                            b_cwuF[ic % 2] + 16 * (ic // 2 + 1))
                                eng.wait_ge(cuev.sem, GIC)
                                last = None
                                for dc in range(c.ndc):
                                    last = eng.matmul(
                                        puF[:],
                                        sWuF[ic % 2][:, dc * PP:(dc + 1) * PP],
                                        sxnbfA[:, dc * SPA:(dc + 1) * SPA],
                                        start=(dc == 0), stop=(dc == c.ndc - 1))
                                cpuF.inc(last)
                    emit("tensor", fTscan)

                    b_swu = [cwuF[0].n, cwuF[1].n]
                    b_swg = [cwgF[0].n, cwgF[1].n]
                    odm = od_odm[k]
                    def fSscan(eng, k=k, GN0=GN0, GI0=GI0, GIC0=GIC0,
                               b_swu=b_swu, b_swg=b_swg, odm=odm):
                        for ic in (range(min(2, c.nih)) if k >= 1 else []):
                            eng.wait_ge(ccast_u.sem, 16 * c.nih)
                            eng.wait_ge(cwuF[ic % 2].sem, b_swu[ic % 2])
                            cwuF[ic % 2].inc(eng.dma_start(out=sWuF[ic % 2][:],
                                                           in_=wub[ic]), 16)
                        for ic in (range(NR, min(NR + 2, c.nih))
                                   if k >= 1 else []):
                            si = ic - NR
                            eng.wait_ge(ccast_g.sem, 16 * (c.nih - NR))
                            eng.wait_ge(cwgF[si % 2].sem, b_swg[si % 2])
                            cwgF[si % 2].inc(eng.dma_start(
                                out=sWgFs[si % 2][:], in_=wgb[si]), 16)
                        for n in range(NST):
                            s = n % 2
                            GN = GN0 + n
                            eng.wait_ge(cbb.sem, GN + 1)
                            if GN >= 2:
                                eng.wait_ge(cadd.sem, GN - 1)
                            eng.activation(sBb[s][:], pB[:], ACT.Copy)
                            cbe.inc(eng.activation(sCb[s][:], pC[:], ACT.Copy))
                            for j in range(c.nhc):
                                GI = GI0 + n * c.nhc + j
                                sl = GI % 3
                                col = j * NST + n
                                if GI >= 3:
                                    eng.wait_ge(cdAf.sem, GI - 2)
                                cdA.inc(eng.activation(
                                    sdA[sl][:], sdt[:, j * SPA:(j + 1) * SPA],
                                    ACT.Exp, scale=sA[:, col:col + 1]))
                            ics = sil_at.get(n, []) if k >= 1 else []
                            if ics:
                                load_act(eng, 18)
                                for ic in ics:
                                    GIC = GIC0 + ic
                                    eng.wait_ge(cpgF.sem, GIC + 1)
                                    if GIC >= 4:
                                        eng.wait_ge(chdnF.sem, GIC - 3)
                                    csgF.inc(eng.activation(
                                        ssgF[GIC % 4][:], pgF[:], ACT.Silu,
                                        scale=sgs[:, ic:ic + 1]))
                                load_act(eng, 6)
                                for ic in ics:
                                    GIC = GIC0 + ic
                                    eng.wait_ge(cpuF.sem, GIC + 1)
                                    if GIC >= 4:
                                        eng.wait_ge(chdnF.sem, GIC - 3)
                                    cuev.inc(eng.activation(
                                        suev[GIC % 4][:], puF[:], ACT.Copy,
                                        scale=sus[:, ic:ic + 1]))
                                    if ic + 2 < c.nih:
                                        eng.wait_ge(ccast_u.sem, 16 * c.nih)
                                        eng.wait_ge(cwuF[ic % 2].sem,
                                                    b_swu[ic % 2] +
                                                    16 * (ic // 2 + 1))
                                        cwuF[ic % 2].inc(eng.dma_start(
                                            out=sWuF[ic % 2][:],
                                            in_=wub[ic + 2]), 16)
                                    if ic >= NR and ic + 2 < c.nih:
                                        si = ic + 2 - NR
                                        eng.wait_ge(ccast_g.sem,
                                                    16 * (c.nih - NR))
                                        eng.wait_ge(cwgF[si % 2].sem,
                                                    b_swg[si % 2] +
                                                    16 * (si // 2))
                                        cwgF[si % 2].inc(eng.dma_start(
                                            out=sWgFs[si % 2][:],
                                            in_=wgb[si]), 16)
                            for p in odm.get(n, []):
                                sp, dp, lo, hi = od_pieces[p]
                                eng.wait_ge(codl[p % 2].sem, 16 * (p // 2 + 1))
                                if p >= 2:
                                    eng.wait_ge(cods[p % 2].sem, 16 * (p // 2))
                                codc.inc(eng.activation(
                                    sob[p % 2][:, 0:hi - lo],
                                    sow[p % 2][:, 0:hi - lo], ACT.Copy))
                                eng.wait_ge(codc.sem, p + 1)
                                cods[p % 2].inc(eng.dma_start(
                                    out=dp[:, lo:hi],
                                    in_=sob[p % 2][:, 0:hi - lo]), 16)
                    emit("scalar", fSscan)

                    def fVscan(eng, k=k, GN0=GN0, GI0=GI0):
                        a_su = su.ap().rearrange("p (nj t) -> p nj t", nj=c.nhc)
                        a_sh = shCr.ap().rearrange("p (nj t) -> p nj t",
                                                   nj=c.nhc)
                        for n in range(NST):
                            GN = GN0 + n
                            eng.wait_ge(cbe.sem, GN + 1)
                            if n == 0:
                                eng.wait_ge(cyd.sem, 16 * c.nhc * k)
                                eng.wait_ge(cu.sem, (k + 1) * c.nhc)
                            if GN >= 1:
                                eng.wait_ge(cadd.sem, GN)
                            b3 = bass.AP(sBb[GN % 2].ap().tensor, 0,
                                         [[SPA, PP], [0, c.nhc], [1, SPA]])
                            cdbx.inc(eng.tensor_tensor(a_sh, a_su, b3,
                                                       AL.mult))
                            for j in range(c.nhc):
                                GI = GI0 + n * c.nhc + j
                                col = j * NST + n
                                eng.wait_ge(cdA.sem, GI + 1)
                                if j == 0:
                                    eng.wait_ge(cdbx.sem, GN + 1)
                                cdAf.inc(eng.tensor_tensor_scan(
                                    shCr[:, j * SPA:(j + 1) * SPA],
                                    sdA[GI % 3][:],
                                    shCr[:, j * SPA:(j + 1) * SPA],
                                    sH[:, col:col + 1], AL.mult, AL.add))
                            eng.wait_ge(cdAf.sem, GI0 + (n + 1) * c.nhc)
                            eng.tensor_copy(sH[:, n::NST],
                                            shCr[:, SPA - 1::SPA])
                            eng.drain()
                            c3 = bass.AP(sCb[GN % 2].ap().tensor, 0,
                                         [[SPA, PP], [0, c.nhc], [1, SPA]])
                            eng.tensor_tensor(a_sh, a_sh, c3, AL.mult)
                            eng.drain()
                            if n == 0:
                                cadd.inc(eng.tensor_copy(sy[:], shCr[:]))
                            else:
                                cadd.inc(eng.tensor_tensor(sy[:], sy[:],
                                                           shCr[:], AL.add))
                    emit("vector", fVscan)

                    def fgpScan(eng, k=k, GN0=GN0, GI0=GI0, GIC0=GIC0):
                        for n in range(NST):
                            GN = GN0 + n
                            if n == 2 and k >= 2:
                                eng.wait_ge(cyd.sem, tot_cyd[k - 2])
                                eng.wait_ge(chd[0].sem, tot_chd0[k - 2])
                                eng.wait_ge(chd[1].sem, tot_chd1[k - 2])
                                cag.inc(eng.collective_compute(
                                    "AllGather", AL.bypass, replica_groups=groups,
                                    ins=[cc_in[k - 2]], outs=[cc_out[k - 2]]))
                                cag.inc(eng.collective_compute(
                                    "AllGather", AL.bypass, replica_groups=groups,
                                    ins=[cc_hin[k - 2]], outs=[cc_hout[k - 2]]))
                            for ic in (sil_gp.get(n, []) if k >= 1 else []):
                                GIC = GIC0 + ic
                                eng.wait_ge(cuev.sem, GIC + 1)
                                eng.wait_ge(csgF.sem, GIC + 1)
                                if GIC >= 2:
                                    eng.wait_ge(chd[GIC % 2].sem, 16 * (GIC // 2))
                                chdnF.inc(eng.tensor_tensor(
                                    shdnF[GIC % 2][:], suev[GIC % 4][:],
                                    ssgF[GIC % 4][:], AL.mult))
                                eng.wait_ge(chdnF.sem, GIC + 1)
                                chd[GIC % 2].inc(eng.dma_start(
                                    out=cc_hin[k][ic * PP:(ic + 1) * PP, :],
                                    in_=shdnF[GIC % 2][:]), 16)
                            for p in od_issue.get(GN, []):
                                sp, dp, lo, hi = od_pieces[p]
                                if p >= 2:
                                    eng.wait_ge(codc.sem, p - 1)
                                codl[p % 2].inc(eng.dma_start(
                                    out=sow[p % 2][:, 0:hi - lo],
                                    in_=sp[:, lo:hi]), 16)
                        eng.wait_ge(cadd.sem, (k + 1) * NST)
                        for j in range(c.nhc):
                            cyd.inc(eng.dma_start(
                                out=cc_in[k][j * PP:(j + 1) * PP, :],
                                in_=sy[:, j * SPA:(j + 1) * SPA]), 16)
                        for m in range(NST, NST + 3):
                            for ic in (sil_gp.get(m, []) if k >= 1 else []):
                                GIC = GIC0 + ic
                                eng.wait_ge(cuev.sem, GIC + 1)
                                eng.wait_ge(csgF.sem, GIC + 1)
                                if GIC >= 2:
                                    eng.wait_ge(chd[GIC % 2].sem, 16 * (GIC // 2))
                                chdnF.inc(eng.tensor_tensor(
                                    shdnF[GIC % 2][:], suev[GIC % 4][:],
                                    ssgF[GIC % 4][:], AL.mult))
                                eng.wait_ge(chdnF.sem, GIC + 1)
                                chd[GIC % 2].inc(eng.dma_start(
                                    out=cc_hin[k][ic * PP:(ic + 1) * PP, :],
                                    in_=shdnF[GIC % 2][:]), 16)
                    emit("gpsimd", fgpScan)

                    if k == 0:
                        def fT0(eng):
                            eng.wait_ge(cgur.sem, 16 * NR)
                            for ic in range(c.nih):
                                if ic < NR:
                                    gsrc = lambda dc, ic=ic: sWgR[
                                        :, (ic * c.ndc + dc) * PP:
                                        (ic * c.ndc + dc + 1) * PP]
                                else:
                                    si = ic - NR
                                    eng.wait_ge(cwgF[si % 2].sem,
                                                16 * (si // 2 + 1))
                                    gsrc = lambda dc, ic=ic: sWgFs[
                                        (ic - NR) % 2][:, dc * PP:(dc + 1) * PP]
                                eng.wait_ge(csgF.sem, ic)
                                last = None
                                for dc in range(c.ndc):
                                    last = eng.matmul(
                                        pgF[:], gsrc(dc),
                                        sxnbfA[:, dc * SPA:(dc + 1) * SPA],
                                        start=(dc == 0), stop=(dc == c.ndc - 1))
                                cpgF.inc(last)
                                eng.wait_ge(cwuF[ic % 2].sem, 16 * (ic // 2 + 1))
                                eng.wait_ge(cuev.sem, ic)
                                last = None
                                for dc in range(c.ndc):
                                    last = eng.matmul(
                                        puF[:],
                                        sWuF[ic % 2][:, dc * PP:(dc + 1) * PP],
                                        sxnbfA[:, dc * SPA:(dc + 1) * SPA],
                                        start=(dc == 0), stop=(dc == c.ndc - 1))
                                cpuF.inc(last)
                        emit("tensor", fT0)

                        def fS0(eng):
                            for ic in range(min(2, c.nih)):
                                eng.wait_ge(ccast_u.sem, 16 * c.nih)
                                cwuF[ic % 2].inc(eng.dma_start(
                                    out=sWuF[ic % 2][:], in_=wub[ic]), 16)
                            for ic in range(NR, min(NR + 2, c.nih)):
                                si = ic - NR
                                eng.wait_ge(ccast_g.sem, 16 * (c.nih - NR))
                                cwgF[si % 2].inc(eng.dma_start(
                                    out=sWgFs[si % 2][:], in_=wgb[si]), 16)
                            for ic in range(c.nih):
                                eng.wait_ge(cpgF.sem, ic + 1)
                                if ic >= 4:
                                    eng.wait_ge(chdnF.sem, ic - 3)
                                load_act(eng, 18)
                                csgF.inc(eng.activation(
                                    ssgF[ic % 4][:], pgF[:], ACT.Silu,
                                    scale=sgs[:, ic:ic + 1]))
                                load_act(eng, 6)
                                eng.wait_ge(cpuF.sem, ic + 1)
                                cuev.inc(eng.activation(
                                    suev[ic % 4][:], puF[:], ACT.Copy,
                                    scale=sus[:, ic:ic + 1]))
                                if ic + 2 < c.nih:
                                    eng.wait_ge(cwuF[ic % 2].sem,
                                                16 * (ic // 2 + 1))
                                    cwuF[ic % 2].inc(eng.dma_start(
                                        out=sWuF[ic % 2][:], in_=wub[ic + 2]), 16)
                                if ic >= NR and ic + 2 < c.nih:
                                    si = ic + 2 - NR
                                    eng.wait_ge(cwgF[si % 2].sem,
                                                16 * (si // 2))
                                    eng.wait_ge(cpgF.sem, ic + 1)
                                    cwgF[si % 2].inc(eng.dma_start(
                                        out=sWgFs[si % 2][:], in_=wgb[si]), 16)
                        emit("scalar", fS0)

                        def fgp0(eng):
                            for ic in range(c.nih):
                                eng.wait_ge(cuev.sem, ic + 1)
                                eng.wait_ge(csgF.sem, ic + 1)
                                if ic >= 2:
                                    eng.wait_ge(chd[ic % 2].sem, 16 * (ic // 2))
                                chdnF.inc(eng.tensor_tensor(
                                    shdnF[ic % 2][:], suev[ic % 4][:],
                                    ssgF[ic % 4][:], AL.mult))
                                eng.wait_ge(chdnF.sem, ic + 1)
                                chd[ic % 2].inc(eng.dma_start(
                                    out=cc_hin[0][ic * PP:(ic + 1) * PP, :],
                                    in_=shdnF[ic % 2][:]), 16)
                        emit("gpsimd", fgp0)

                    tot_cyd.append(cyd.n)
                    tot_chd0.append(chd[0].n)
                    tot_chd1.append(chd[1].n)

                def fgpAGl(eng):
                    eng.wait_ge(cyd.sem, cyd.n)
                    eng.wait_ge(chd[0].sem, chd[0].n)
                    eng.wait_ge(chd[1].sem, chd[1].n)
                    for k2 in range(max(0, c.NSP - 2), c.NSP):
                        cag.inc(eng.collective_compute(
                            "AllGather", AL.bypass, replica_groups=groups,
                            ins=[cc_in[k2]], outs=[cc_out[k2]]))
                        cag.inc(eng.collective_compute(
                            "AllGather", AL.bypass, replica_groups=groups,
                            ins=[cc_hin[k2]], outs=[cc_hout[k2]]))
                emit("gpsimd", fgpAGl)
                barrier()

        cw, ncg = c.cwB, c.ncgB
        for sidx in range(c.nown):
            colB = sidx * SPB
            with contextlib.ExitStack() as esB:
                sbB = lambda name, shape, dt: esB.enter_context(
                    nc.sbuf_tensor(name, shape, dt))
                syfull = sbB(f"syfull{sidx}", [PP, c.ndc * SPB], BF16)
                shdn = sbB(f"shdn{sidx}", [PP, c.nic * SPB], BF16)
                scca = [sbB(f"scca{sidx}_{i}", [PP, SPA], BF16) for i in range(2)]
                sccb = [sbB(f"sccb{sidx}_{i}", [PP, SPA], BF16) for i in range(2)]
                cld = [ch("cld0"), ch("cld1")]
                cyf = ch("cyf")
                b_cld = [cld[0].n, cld[1].n]
                b_cyf = cyf.n
                items = []
                for g in range(c.GPS):
                    ka = 2 * (sidx * c.GPS + g)
                    for dc in range(c.ndc):
                        items.append((cc_out[ka], cc_out[ka + 1], dc,
                                      syfull, dc * SPB + g * SPA))
                    for ic in range(c.nic):
                        items.append((cc_hout[ka], cc_hout[ka + 1], ic,
                                      shdn, ic * SPB + g * SPA))

                def fsyncB1(eng, items=items):
                    eng.wait_ge(cag.sem, cag.n)
                    for t, (ta, tb, chunk, dst, off) in enumerate(items):
                        s = t % 2
                        if t >= 2:
                            eng.wait_ge(cyf.sem, b_cyf + t - 1)
                        cld[s].inc(eng.dma_start(
                            out=scca[s][:],
                            in_=ta[chunk * PP:(chunk + 1) * PP, :]), 16)
                        cld[s].inc(eng.dma_start(
                            out=sccb[s][:],
                            in_=tb[chunk * PP:(chunk + 1) * PP, :]), 16)
                emit("sync", fsyncB1)

                def fVB1(eng, items=items):
                    for t, (ta, tb, chunk, dst, off) in enumerate(items):
                        s = t % 2
                        eng.wait_ge(cld[s].sem, b_cld[s] + 32 * (t // 2 + 1))
                        eng.tensor_scalar(scca[s][:], scca[s][:], smks[:, 0:1],
                                          None, AL.mult)
                        eng.tensor_scalar(sccb[s][:], sccb[s][:], smks[:, 1:2],
                                          None, AL.mult)
                        eng.drain()
                        cyf.inc(eng.tensor_tensor(dst[:, off:off + SPA],
                                                  scca[s][:], sccb[s][:], AL.add))
                emit("vector", fVB1)
                barrier()

                sWo = [sbB(f"sWo{sidx}_{i}", [PP, c.ndc * PP], BF16) for i in range(2)]
                sWdn = [sbB(f"sWdn{sidx}_{i}", [PP, c.nic * PP], BF16) for i in range(2)]
                sxres = [sbB(f"sxres{sidx}_{i}", [PP, SPB], F32) for i in range(2)]
                so1 = [sbB(f"so1{sidx}_{i}", [PP, SPB], F32) for i in range(2)]
                sout_t = [sbB(f"sout{sidx}_{i}", [PP, SPB], F32) for i in range(2)]
                with contextlib.ExitStack() as ps:
                    po = [ps.enter_context(nc.psum_tensor(
                        f"po{sidx}_{i}", [PP, SPB], F32)) for i in range(2)]
                    pd = [ps.enter_context(nc.psum_tensor(
                        f"pd{sidx}_{i}", [PP, SPB], F32)) for i in range(2)]
                    cwq = [ch("cwq0"), ch("cwq1"), ch("cwq2")]
                    cxr = [ch("cxr0"), ch("cxr1")]
                    cpo, cpd = ch("cpo"), ch("cpd")
                    co1, cout = ch("co1"), ch("cout")
                    sto = [ch("sto0"), ch("sto1")]
                    b_cwq = [cwq[0].n, cwq[1].n, cwq[2].n]
                    b_cxr = [cxr[0].n, cxr[1].n]
                    b_cpo, b_cpd = cpo.n, cpd.n
                    b_co1, b_cout = co1.n, cout.n
                    b_sto = [sto[0].n, sto[1].n]
                    qof, qcnt = {}, [0, 0, 0]
                    for mc in range(c.nmc):
                        q = mc % 3
                        qcnt[q] += 32
                        qof[mc] = (q, qcnt[q])

                    def wload1(eng, q, mc):
                        eng.wait_ge(cods[0].sem, cods[0].n)
                        eng.wait_ge(cods[1].sem, cods[1].n)
                        if mc >= 2:
                            eng.wait_ge(cpd.sem, b_cpd + mc - 1)
                        cwq[q].inc(eng.dma_start(out=sWo[mc % 2][:],
                                                 in_=woutb[mc]), 16)
                        cwq[q].inc(eng.dma_start(out=sWdn[mc % 2][:],
                                                 in_=wdb[mc]), 16)

                    def fsyncB3(eng):
                        for mc in range(c.nmc):
                            s = mc % 2
                            if mc >= 2:
                                eng.wait_ge(co1.sem, b_co1 + mc - 1)
                            cxr[s].inc(eng.dma_start(
                                out=sxres[s][:],
                                in_=xB[mc * PP:(mc + 1) * PP, colB:colB + SPB]), 16)
                            if mc % 3 == 0:
                                wload1(eng, 0, mc)
                    emit("sync", fsyncB3)

                    def factB3(eng):
                        for mc in range(c.nmc):
                            if mc >= 2:
                                eng.wait_ge(cout.sem, b_cout + mc - 1)
                                sto[mc % 2].inc(eng.dma_start(
                                    out=out[(mc - 2) * PP:(mc - 1) * PP,
                                            colB:colB + SPB],
                                    in_=sout_t[mc % 2][:]), 16)
                            if mc % 3 == 1:
                                wload1(eng, 1, mc)
                        for mc in range(c.nmc - 2, c.nmc):
                            eng.wait_ge(cout.sem, b_cout + mc + 1)
                            sto[mc % 2].inc(eng.dma_start(
                                out=out[mc * PP:(mc + 1) * PP, colB:colB + SPB],
                                in_=sout_t[mc % 2][:]), 16)
                        eng.wait_ge(sto[0].sem, sto[0].n)
                        eng.wait_ge(sto[1].sem, sto[1].n)
                    emit("scalar", factB3)

                    def fgpB3(eng):
                        for mc in range(c.nmc):
                            if mc % 3 == 2:
                                wload1(eng, 2, mc)
                    emit("gpsimd", fgpB3)

                    def fTB3(eng):
                        for mc in range(c.nmc):
                            s = mc % 2
                            q, cnt = qof[mc]
                            eng.wait_ge(cwq[q].sem, b_cwq[q] + cnt)
                            if mc >= 2:
                                eng.wait_ge(cout.sem, b_cout + mc - 1)
                                eng.wait_ge(co1.sem, b_co1 + mc - 1)
                            last = None
                            for dc in range(c.ndc):
                                for g in range(ncg):
                                    last = eng.matmul(
                                        po[s][:, g * cw:(g + 1) * cw],
                                        sWo[s][:, dc * PP:(dc + 1) * PP],
                                        syfull[:, dc * SPB + g * cw: dc * SPB + (g + 1) * cw],
                                        start=(dc == 0), stop=(dc == c.ndc - 1))
                            cpo.inc(last)
                            last = None
                            for ic in range(c.nic):
                                for g in range(ncg):
                                    last = eng.matmul(
                                        pd[s][:, g * cw:(g + 1) * cw],
                                        sWdn[s][:, ic * PP:(ic + 1) * PP],
                                        shdn[:, ic * SPB + g * cw: ic * SPB + (g + 1) * cw],
                                        start=(ic == 0), stop=(ic == c.nic - 1))
                            cpd.inc(last)
                    emit("tensor", fTB3)

                    def fVB3(eng):
                        for mc in range(c.nmc):
                            s = mc % 2
                            eng.wait_ge(cpd.sem, b_cpd + mc + 1)
                            eng.wait_ge(cxr[s].sem, b_cxr[s] + 16 * (mc // 2 + 1))
                            eng.wait_ge(cpo.sem, b_cpo + mc + 1)
                            if mc >= 2:
                                eng.wait_ge(sto[s].sem, b_sto[s] + 16 * ((mc - 2) // 2 + 1))
                            co1.inc(eng.scalar_tensor_tensor(
                                so1[s][:], pd[s][:], sds[:, mc:mc + 1], sxres[s][:],
                                AL.mult, AL.add))
                            eng.drain()
                            cout.inc(eng.scalar_tensor_tensor(
                                sout_t[s][:], po[s][:], soutb[:, mc:mc + 1], so1[s][:],
                                AL.add, AL.add))
                    emit("vector", fVB3)
                    barrier()

        sto0, sto1 = ch("sto0"), ch("sto1")
        def ffinal(eng):
            eng.wait_ge(sto0.sem, sto0.n)
            eng.wait_ge(sto1.sem, sto1.n)
        emit("sync", ffinal)
    return nc



def pack_inputs(cfg: Cfg, core_id, x, A_log, x_proj_w, x_proj_b, dt_proj_w, dt_proj_b,
                out_proj_w, out_proj_b, gate_w, up_w, down_w, gate_s, up_s, down_s,
                shared_cache=None):
    c = cfg
    seq, half = core_id // 2, core_id % 2
    xb = np.ascontiguousarray(x[seq].T)
    dlo, dhi = half * c.DH, (half + 1) * c.DH
    own_blocks = [2 * i + half for i in range(c.NSP // 2)]
    xBv = np.ascontiguousarray(
        np.concatenate([xb[:, k * c.SPA:(k + 1) * c.SPA] for k in own_blocks], axis=1))

    def blob(wT):
        K, M = wT.shape
        a = wT.reshape(K // PP, PP, M // PP, PP)
        return np.ascontiguousarray(
            a.transpose(2, 1, 0, 3).reshape(M // PP, PP, (K // PP) * PP))

    sc = shared_cache if shared_cache is not None else {}
    if "wdt0" not in sc:
        for h in (0, 1):
            sc[f"wdt{h}"] = np.ascontiguousarray(
                blob(dt_proj_w.T[:, h * c.DH:(h + 1) * c.DH]), dtype=np.float32)
        a = x_proj_w.T.reshape(c.ndc, PP, 2 * c.NST)
        sc["wxp"] = np.ascontiguousarray(
            a.transpose(1, 0, 2).reshape(PP, c.ndc * 2 * c.NST), dtype=np.float32)
        sc["wout"] = np.ascontiguousarray(blob(out_proj_w.T), dtype=np.float32)
        gb, ub = blob(gate_w.T), blob(up_w.T)
        for h in (0, 1):
            sc[f"wg{h}"] = np.ascontiguousarray(
                gb[h * c.nih:(h + 1) * c.nih], dtype=np.float32)
            sc[f"wu{h}"] = np.ascontiguousarray(
                ub[h * c.nih:(h + 1) * c.nih], dtype=np.float32)
        sc["wd"] = np.ascontiguousarray(blob(down_w.T), dtype=np.float32)
    wdtv = sc[f"wdt{half}"]
    wxpv, woutv, wdv = sc["wxp"], sc["wout"], sc["wd"]
    wgv, wuv = sc[f"wg{half}"], sc[f"wu{half}"]

    alogv = np.ascontiguousarray(
        A_log[dlo:dhi].reshape(c.nhc, PP, c.NST).transpose(1, 0, 2).reshape(
            PP, c.nhc * c.NST))
    dtbv = np.ascontiguousarray(dt_proj_b[dlo:dhi].reshape(c.nhc, PP).T)
    xpbv = x_proj_b.reshape(2 * c.NST, 1)
    outbv = np.ascontiguousarray(out_proj_b.reshape(c.nmc, PP).T)
    dscv = np.ascontiguousarray(down_s.reshape(c.nmc, PP).T)
    ilo, ihi = half * c.nih, (half + 1) * c.nih
    gscv = np.ascontiguousarray(gate_s.reshape(c.nic, PP)[ilo:ihi].T)
    uscv = np.ascontiguousarray(up_s.reshape(c.nic, PP)[ilo:ihi].T)
    mksv = np.zeros((PP, 2), np.float32)
    mksv[:, 0] = 1.0 if half == 0 else 0.0
    mksv[:, 1] = 0.0 if half == 0 else 1.0

    f32 = lambda a: np.ascontiguousarray(a, dtype=np.float32)
    return {
        "x_fm": f32(xb), "x_h": f32(xb[dlo:dhi]), "xB": f32(xBv),
        "wdt": wdtv, "wxp": wxpv, "wout": woutv, "wg": wgv, "wu": wuv, "wd": wdv,
        "alog": f32(alogv), "dtb": f32(dtbv), "xpb": f32(xpbv), "outb": f32(outbv),
        "dsc": f32(dscv), "gsc": f32(gscv), "usc": f32(uscv), "mks": f32(mksv),
    }


def assemble_output(cfg: Cfg, results):
    c = cfg
    B = len(results) // 2
    full = np.empty((B, c.L, c.D), np.float32)
    for core_id, r in enumerate(results):
        seq, half = core_id // 2, core_id % 2
        o = np.asarray(r["out"])
        own_blocks = [2 * i + half for i in range(c.NSP // 2)]
        for bi, k in enumerate(own_blocks):
            full[seq, k * c.SPA:(k + 1) * c.SPA, :] = \
                o[:, bi * c.SPA:(bi + 1) * c.SPA].T
    return full



_CACHE = {}


def _get_nc(cfg):
    key = (cfg.D, cfg.L, cfg.INTER, cfg.SPA, cfg.SPB, cfg.n_cores)
    if key not in _CACHE:
        _CACHE[key] = build(cfg)
    return _CACHE[key]


def kernel(L0_attn_x, L0_attn_A_log, L0_attn_x_proj_w, L0_attn_x_proj_b,
           L0_attn_dt_proj_w, L0_attn_dt_proj_b, L0_attn_out_proj_w,
           L0_attn_out_proj_b, L0_ffn_gate_w, L0_ffn_up_w, L0_ffn_down_w,
           L0_ffn_gate_s, L0_ffn_up_s, L0_ffn_down_s):
    from concourse.bass_utils import run_bass_kernel_spmd

    cfg = Cfg(D=2048, L=4096, INTER=5632, SPA=512, SPB=1024, NST=16, n_cores=8)
    args = (np.asarray(L0_attn_x, np.float32),
            np.asarray(L0_attn_A_log, np.float32),
            np.asarray(L0_attn_x_proj_w, np.float32),
            np.asarray(L0_attn_x_proj_b, np.float32),
            np.asarray(L0_attn_dt_proj_w, np.float32),
            np.asarray(L0_attn_dt_proj_b, np.float32),
            np.asarray(L0_attn_out_proj_w, np.float32),
            np.asarray(L0_attn_out_proj_b, np.float32),
            np.asarray(L0_ffn_gate_w, np.float32),
            np.asarray(L0_ffn_up_w, np.float32),
            np.asarray(L0_ffn_down_w, np.float32),
            np.asarray(L0_ffn_gate_s, np.float32),
            np.asarray(L0_ffn_up_s, np.float32),
            np.asarray(L0_ffn_down_s, np.float32))

    shared = {}
    in_maps = [pack_inputs(cfg, cid, *args, shared_cache=shared)
               for cid in range(cfg.n_cores)]
    nc = _get_nc(cfg)
    res = run_bass_kernel_spmd(nc, in_maps, core_ids=list(range(cfg.n_cores)))
    return assemble_output(cfg, res.results)


# version 16
# speedup vs baseline: 1.0382x; 1.0382x over previous
import contextlib
import numpy as np
import concourse.bass as bass
import concourse.mybir as mybir

F32 = mybir.dt.float32
BF16 = mybir.dt.bfloat16
AL = mybir.AluOpType
ACT = mybir.ActivationFunctionType

PP = 128
AG_ON_SYNC = True


class Cfg:
    def __init__(self, D=2048, L=4096, INTER=5632, SPA=512, SPB=1024, NST=16,
                 n_cores=8):
        self.D, self.L, self.INTER, self.NST = D, L, INTER, NST
        self.SPA, self.SPB = SPA, SPB
        self.n_cores = n_cores
        self.DH = D // 2
        self.NSP = L // SPA
        self.ndc = D // PP
        self.nhc = self.DH // PP
        self.nic = INTER // PP
        self.nih = self.nic // 2
        self.nmc = self.ndc
        self.cwA = min(512, SPA)
        self.ncgA = SPA // self.cwA
        self.cwB = min(512, SPB)
        self.ncgB = SPB // self.cwB
        self.GPS = SPB // SPA
        self.nown = L // (2 * SPB)
        assert L % SPA == 0 and self.NSP % 2 == 0 and SPB % SPA == 0
        assert D % 256 == 0 and INTER % (2 * PP) == 0 and L % (2 * SPB) == 0
        self.eps = 1e-5


def build(cfg: Cfg):
    nc = bass.Bass("TRN2", target_bir_lowering=False)
    c = cfg
    SPA, SPB, NST = c.SPA, c.SPB, c.NST

    x_fm = nc.dram_tensor("x_fm", [c.D, c.L], F32, kind="ExternalInput").ap()
    x_h = nc.dram_tensor("x_h", [c.DH, c.L], F32, kind="ExternalInput").ap()
    xB = nc.dram_tensor("xB", [c.D, c.nown * SPB], F32, kind="ExternalInput").ap()
    wdt = nc.dram_tensor("wdt", [c.nhc, PP, c.ndc * PP], F32, kind="ExternalInput").ap()
    wxp = nc.dram_tensor("wxp", [PP, c.ndc * 2 * NST], F32, kind="ExternalInput").ap()
    wout = nc.dram_tensor("wout", [c.nmc, PP, c.ndc * PP], F32, kind="ExternalInput").ap()
    wg = nc.dram_tensor("wg", [c.nih, PP, c.ndc * PP], F32, kind="ExternalInput").ap()
    wu = nc.dram_tensor("wu", [c.nih, PP, c.ndc * PP], F32, kind="ExternalInput").ap()
    wd = nc.dram_tensor("wd", [c.nmc, PP, c.nic * PP], F32, kind="ExternalInput").ap()
    alog = nc.dram_tensor("alog", [PP, c.nhc * NST], F32, kind="ExternalInput").ap()
    dtb = nc.dram_tensor("dtb", [PP, c.nhc], F32, kind="ExternalInput").ap()
    xpb = nc.dram_tensor("xpb", [2 * NST, 1], F32, kind="ExternalInput").ap()
    outb = nc.dram_tensor("outb", [PP, c.nmc], F32, kind="ExternalInput").ap()
    dsc = nc.dram_tensor("dsc", [PP, c.nmc], F32, kind="ExternalInput").ap()
    gsc = nc.dram_tensor("gsc", [PP, c.nih], F32, kind="ExternalInput").ap()
    usc = nc.dram_tensor("usc", [PP, c.nih], F32, kind="ExternalInput").ap()
    mks = nc.dram_tensor("mks", [PP, 2], F32, kind="ExternalInput").ap()
    out = nc.dram_tensor("out", [c.D, c.nown * SPB], F32, kind="ExternalOutput").ap()
    cc_in = [nc.dram_tensor(f"cc_in{k}", [c.DH, SPA], BF16, kind="Internal").ap()
             for k in range(c.NSP)]
    cc_out = [nc.dram_tensor(f"cc_out{k}", [c.D, SPA], BF16, kind="Internal").ap()
              for k in range(c.NSP)]
    wdtb = nc.dram_tensor("wdtb", [c.nhc, PP, c.ndc * PP], BF16, kind="Internal").ap()
    wub = nc.dram_tensor("wub", [c.nih, PP, c.ndc * PP], BF16, kind="Internal").ap()
    woutb = nc.dram_tensor("woutb", [c.nmc, PP, c.ndc * PP], BF16, kind="Internal").ap()
    wdb = nc.dram_tensor("wdb", [c.nmc, PP, c.nic * PP], BF16, kind="Internal").ap()
    wgb = nc.dram_tensor(
        "wgb", [max(1, c.nih - max(1, (c.nih + 1) // 2 - 6)), PP, c.ndc * PP],
        BF16, kind="Internal").ap()
    cc_hin = [nc.dram_tensor(f"cc_hin{k}", [c.nih * PP, SPA], BF16, kind="Internal").ap()
              for k in range(c.NSP)]
    cc_hout = [nc.dram_tensor(f"cc_hout{k}", [c.nic * PP, SPA], BF16, kind="Internal").ap()
               for k in range(c.NSP)]
    groups = [[2 * p, 2 * p + 1] for p in range(max(1, c.n_cores // 2))]

    es = contextlib.ExitStack()
    with es:
        sb = lambda name, shape, dt: es.enter_context(nc.sbuf_tensor(name, shape, dt))
        sA = sb("sA", [PP, c.nhc * NST], F32)
        sK = sb("sK", [PP, 3], F32)
        sdtb = sb("sdtb", [PP, c.nhc], F32)
        sxpb = sb("sxpb", [2 * NST, 1], F32)
        soutb = sb("soutb", [PP, c.nmc], F32)
        sds = sb("sds", [PP, c.nmc], F32)
        sgs = sb("sgs", [PP, c.nih], F32)
        sus = sb("sus", [PP, c.nih], F32)
        smks = sb("smks", [PP, 2], F32)
        sWxp = sb("sWxp", [PP, c.ndc * 2 * NST], BF16)
        sones_c = sb("sones_c", [PP, 1], F32)
        sones_r = sb("sones_r", [96, PP], F32)
        sH = sb("sH", [PP, c.nhc * NST], F32)

        class Ch:
            def __init__(self, name):
                self.sem = es.enter_context(nc.semaphore(name))
                self.n = 0
            def inc(self, instr, k=1):
                instr.then_inc(self.sem, k)
                self.n += k
                return instr
        chs = {}
        def ch(name):
            if name not in chs:
                chs[name] = Ch(name)
            return chs[name]

        block = es.enter_context(nc.Block())
        ENG = ["tensor", "vector", "scalar", "gpsimd", "sync"]

        def emit(eng, fn):
            getattr(block, eng)(fn)

        bsem = ch("bsem")
        def barrier():
            bsem.n += 5
            target = bsem.n
            for e in ENG:
                def f(eng, _t=target):
                    eng.drain().then_inc(bsem.sem, 1)
                    eng.wait_ge(bsem.sem, _t)
                emit(e, f)

        def load_act(eng, idx):
            eng.add_instruction(mybir.InstLoadActFuncSet(
                name=nc.get_next_instruction_name(), act_func_set_id=idx,
                ins=[], outs=[]))

        ld_c = ch("ld_c")
        ld_g = ch("ld_g")
        ccast_dt = ch("ccast_dt")
        def pre_sync(eng):
            for dst, src in [(sA[:], alog), (sdtb[:], dtb), (sxpb[:], xpb),
                             (soutb[:], outb), (sds[:], dsc), (sgs[:], gsc),
                             (sus[:], usc), (smks[:], mks)]:
                ld_c.inc(eng.dma_start(out=dst, in_=src), 16)
        emit("sync", pre_sync)
        def pre_gp(eng):
            ld_g.inc(eng.dma_start(out=sWxp[:], in_=wxp), 16)
            for ec in range(c.nhc):
                ccast_dt.inc(eng.dma_start(out=wdtb[ec], in_=wdt[ec]), 16)
        emit("gpsimd", pre_gp)
        def pre_v(eng):
            eng.memset(sones_c[:], 1.0)
            eng.memset(sones_r[:], 1.0)
            eng.memset(sH[:], 0.0)
            eng.memset(sK[:, 0:1], float(c.eps))
            eng.memset(sK[:, 1:2], 1.0)
            eng.memset(sK[:, 2:3], -0.5)
        emit("vector", pre_v)
        sA_s = ch("sA_s")
        def pre_s(eng):
            load_act(eng, 6)
            eng.wait_ge(ld_c.sem, ld_c.n)
            sA_s.inc(eng.activation(sA[:], sA[:], ACT.Exp))
        emit("scalar", pre_s)
        def pre_v2(eng):
            eng.wait_ge(sA_s.sem, sA_s.n)
            eng.tensor_scalar_mul(sA[:], sA[:], -1.0)
        emit("vector", pre_v2)
        barrier()

        cag = ch("cag")

        cw, ncg = c.cwA, c.ncgA
        NIT = NST * c.nhc
        with contextlib.ExitStack() as esA:
            sbA = lambda name, shape, dt: esA.enter_context(nc.sbuf_tensor(name, shape, dt))
            sxc = [sbA(f"sxc{i}", [PP, SPA], F32) for i in range(2)]
            sxn = [sbA(f"sxn{i}", [PP, SPA], F32) for i in range(4)]
            st1 = [sbA(f"st1_{i}", [PP, SPA], F32) for i in range(2)]
            stmp = sbA("stmp", [PP, SPA], F32)
            srs_b = sbA("srs_b", [PP, SPA], F32)
            smrs_b = sbA("smrs_b", [PP, SPA], F32)
            sxnbfA = sbA("sxnbfA", [PP, c.ndc * SPA], BF16)
            sdt = sbA("sdt", [PP, c.nhc * SPA], F32)
            su = sbA("su", [PP, c.nhc * SPA], F32)
            sy = sbA("sy", [PP, c.nhc * SPA], F32)
            sWdt = [sbA(f"sWdt{i}", [PP, c.ndc * PP], BF16) for i in range(2)]
            sbc2 = sbA("sbc2", [96, 5 * SPA], F32)
            srs = sbc2[0:1, SPA:2 * SPA]
            smrs = sbc2[0:1, 2 * SPA:3 * SPA]
            sm_row = sbc2[0:1, 3 * SPA:4 * SPA]
            svar = sbc2[0:1, 4 * SPA:5 * SPA]
            sBb = [sbA(f"sBb{i}", [PP, SPA], F32) for i in range(2)]
            sCb = [sbA(f"sCb{i}", [PP, SPA], F32) for i in range(2)]
            sdA = [sbA(f"sdA{i}", [PP, SPA], F32) for i in range(4)]
            sdBx = [sbA(f"sdBx{i}", [PP, SPA], F32) for i in range(2)]
            sh_ = [sbA(f"sh{i}", [PP, SPA], F32) for i in range(2)]
            shCr = sbA("shCr", [PP, c.nhc * SPA], F32)
            NR = max(1, (c.nih + 1) // 2 - 6)
            sWgR = sbA("sWgR", [PP, NR * c.ndc * PP], BF16)
            sWgFs = [sbA(f"sWgFs{i}", [PP, c.ndc * PP], BF16) for i in range(2)]
            sWuF = [sbA(f"sWuF{i}", [PP, c.ndc * PP], BF16) for i in range(2)]
            ssgF = [sbA(f"ssgF{i}", [PP, SPA], F32) for i in range(4)]
            suev = [sbA(f"suev{i}", [PP, SPA], F32) for i in range(4)]
            shdnF = [sbA(f"shdnF{i}", [PP, SPA], BF16) for i in range(2)]
            ODW = 384
            sow = [sbA(f"sow{i}", [PP, ODW], F32) for i in range(2)]
            sob = [sbA(f"sob{i}", [PP, ODW], BF16) for i in range(2)]

            psA = contextlib.ExitStack()
            with psA:
                ps0 = psA.enter_context(nc.psum_tensor("ps0", [1, SPA], F32))
                ps1 = psA.enter_context(nc.psum_tensor("ps1", [1, SPA], F32))
                prep = psA.enter_context(nc.psum_tensor("prep", [PP, SPA], F32))
                pdt = psA.enter_context(nc.psum_tensor("pdt", [PP, SPA], F32))
                pB = psA.enter_context(nc.psum_tensor("pB", [PP, SPA], F32))
                pC = psA.enter_context(nc.psum_tensor("pC", [PP, SPA], F32))
                pgF = psA.enter_context(nc.psum_tensor("pgF", [PP, SPA], F32))
                puF = psA.enter_context(nc.psum_tensor("puF", [PP, SPA], F32))

                ldx = [ch("ldx0"), ch("ldx1")]
                ldn = [ch("ldn0"), ch("ldn1")]
                xmm, xsq, sqmm = ch("xmm"), ch("xsq"), ch("sqmm")
                rv, rs_ = ch("rowV"), ch("rowS")
                bb, be = ch("bb"), ch("be")
                vn, cu = ch("vnorm"), ch("cu")
                cwdt = [ch("cwdt0"), ch("cwdt1")]
                cpdt, cdtS = ch("cpdt"), ch("cdtS")
                cxp, cxpS = ch("cxp"), ch("cxpS")
                crow = [ch("crow0"), ch("crow1")]
                cbb, cbe = ch("cbb"), ch("cbe")
                cdA, cdAf = ch("cdA"), ch("cdAf")
                cadd = ch("cadd")
                chC, chCf = ch("chC"), ch("chCf")
                cdbx = ch("cdbx")
                cgur = ch("cgur")
                cwgF = [ch("cwgF0"), ch("cwgF1")]
                cwuF = [ch("cwuF0"), ch("cwuF1")]
                cpgF, cpuF = ch("cpgF"), ch("cpuF")
                csgF, cuev = ch("csgF"), ch("cuev")
                chdnF = ch("chdnF")
                chd = [ch("chd0"), ch("chd1")]
                cyd = ch("cyd")
                codl = [ch("codl0"), ch("codl1")]
                codc = ch("codc")
                cods = [ch("cods0"), ch("cods1")]
                NJG = max(1, c.nhc // 2)
                NADD = c.nhc // NJG

                nspan = max(1, min(NST - 4, 11))
                sil_at = {}
                for ic in range(c.nih):
                    nslot = 1 + (ic * nspan) // c.nih
                    sil_at.setdefault(nslot, []).append(ic)
                sil_gp = {n + 1: ics for n, ics in sil_at.items()}

                def chopA(total):
                    o, r = [], 0
                    while r < total:
                        o.append((r, min(r + ODW, total)))
                        r += ODW
                    return o
                od_pieces = []
                for mc in range(c.nmc):
                    for lo, hi in chopA(c.ndc * PP):
                        od_pieces.append((wout[mc], woutb[mc], lo, hi))
                    for lo, hi in chopA(c.nic * PP):
                        od_pieces.append((wd[mc], wdb[mc], lo, hi))
                NOD = len(od_pieces)
                od_cm = {}
                od_odm = {}
                for k2 in range(c.NSP):
                    p0, p1 = k2 * NOD // c.NSP, (k2 + 1) * NOD // c.NSP
                    m = {}
                    for idx in range(p1 - p0):
                        slot = min(NST - 1,
                                   1 + idx * (NST - 1) // max(1, p1 - p0))
                        m.setdefault(slot, []).append(p0 + idx)
                        od_cm[p0 + idx] = k2 * NST + slot
                    od_odm[k2] = m
                od_issue = {}
                for p in range(NOD):
                    gs = max(0, od_cm[p] - 2,
                             od_cm.get(p - 2, 0))
                    od_issue.setdefault(gs, []).append(p)


                ccast_u = ch("ccast_u")
                ccast_g = ch("ccast_g")
                def fgpPreA(eng):
                    for ic in range(c.nih):
                        if ic < NR:
                            cgur.inc(eng.dma_start(
                                out=sWgR[:, ic * c.ndc * PP:(ic + 1) * c.ndc * PP],
                                in_=wg[ic]), 16)
                        else:
                            ccast_g.inc(eng.dma_start(out=wgb[ic - NR],
                                                      in_=wg[ic]), 16)
                        ccast_u.inc(eng.dma_start(out=wub[ic], in_=wu[ic]), 16)
                emit("gpsimd", fgpPreA)

                tot_cyd, tot_chd0, tot_chd1 = [], [], []
                for k in range(c.NSP):
                    col0 = k * SPA
                    EC0 = k * c.nhc
                    GN0 = k * NST
                    GI0 = k * NIT
                    GIC0 = k * c.nih

                    b_ldS = [ldn[0].n, ldn[1].n]
                    b_xmm, b_xsq, b_sqmm = xmm.n, xsq.n, sqmm.n
                    def fsyncS(eng, k=k, col0=col0, b_xmm=b_xmm,
                               b_sqmm=b_sqmm, b_ldS=b_ldS):
                        for dc in range(c.ndc):
                            s = dc % 2
                            if dc < 4:
                                eng.wait_ge(vn.sem, vn.n)
                            else:
                                eng.wait_ge(xmm.sem, b_xmm + ncg * (dc - 3))
                                eng.wait_ge(sqmm.sem, b_sqmm + ncg * (dc - 3))
                            if dc >= 2:
                                eng.wait_ge(ldn[s].sem, b_ldS[s] + 16 * (dc // 2))
                            else:
                                eng.wait_ge(ldn[s].sem, b_ldS[s])
                            ldn[s].inc(eng.dma_start(
                                out=sxn[dc % 4][:],
                                in_=x_fm[dc * PP:(dc + 1) * PP, col0:col0 + SPA]), 16)
                    emit("sync", fsyncS)

                    def fTS(eng, k=k, b_ldS=b_ldS, b_xsq=b_xsq):
                        eng.wait_ge(rv.sem, rv.n)
                        for dc in range(c.ndc):
                            s = dc % 2
                            eng.wait_ge(ldn[s].sem, b_ldS[s] + 16 * (dc // 2 + 1))
                            for g in range(ncg):
                                xmm.inc(eng.matmul(ps0[:, g * cw:(g + 1) * cw],
                                                   sones_c[:],
                                                   sxn[dc % 4][:, g * cw:(g + 1) * cw],
                                                   start=(dc == 0), stop=(dc == c.ndc - 1)))
                            eng.wait_ge(xsq.sem, b_xsq + dc + 1)
                            for g in range(ncg):
                                sqmm.inc(eng.matmul(ps1[:, g * cw:(g + 1) * cw],
                                                    sones_c[:],
                                                    st1[s][:, g * cw:(g + 1) * cw],
                                                    start=(dc == 0), stop=(dc == c.ndc - 1)))
                    emit("tensor", fTS)

                    def fSS(eng, k=k, b_ldS=b_ldS, b_sqmm=b_sqmm):
                        eng.wait_ge(cu.sem, cu.n)
                        for dc in range(c.ndc):
                            s = dc % 2
                            eng.wait_ge(ldn[s].sem, b_ldS[s] + 16 * (dc // 2 + 1))
                            if dc >= 2:
                                eng.wait_ge(sqmm.sem, b_sqmm + ncg * (dc - 1))
                            xsq.inc(eng.activation(st1[s][:], sxn[dc % 4][:],
                                                   ACT.Square))
                    emit("scalar", fSS)

                    b_rv, b_rs, b_bb, b_be = rv.n, rs_.n, bb.n, be.n
                    def frowV(eng, b_rs=b_rs):
                        eng.wait_ge(xmm.sem, xmm.n)
                        eng.wait_ge(sqmm.sem, sqmm.n)
                        eng.tensor_scalar_mul(sm_row, ps0[0:1, :], 1.0 / c.D)
                        eng.tensor_scalar_mul(svar, ps1[0:1, :], 1.0 / c.D)
                        eng.drain()
                        eng.tensor_tensor(stmp[0:1, :], sm_row,
                                          sm_row, AL.mult)
                        eng.drain()
                        rv.inc(eng.tensor_tensor(svar, svar,
                                                 stmp[0:1, :], AL.subtract))
                        eng.wait_ge(rs_.sem, b_rs + 1)
                        rv.inc(eng.tensor_tensor(smrs, sm_row,
                                                 srs, AL.mult))
                    emit("vector", frowV)

                    def frowS(eng, b_rv=b_rv, b_bb=b_bb):
                        eng.wait_ge(rv.sem, b_rv + 1)
                        eng.activation(svar, svar, ACT.Ln,
                                       bias=sK[0:1, 0:1])
                        eng.drain()
                        rs_.inc(eng.activation(srs, svar,
                                               ACT.Exp, scale=sK[0:1, 2:3]))
                        eng.wait_ge(bb.sem, b_bb + 1)
                        be.inc(eng.activation(srs_b[:], prep[:], ACT.Copy))
                        eng.wait_ge(bb.sem, b_bb + 2)
                        be.inc(eng.activation(smrs_b[:], prep[:], ACT.Copy))
                    emit("scalar", frowS)

                    def fbT(eng, b_rv=b_rv, b_be=b_be):
                        eng.wait_ge(rv.sem, b_rv + 2)
                        eng.wait_ge(cu.sem, cu.n)
                        eng.wait_ge(cxpS.sem, cxpS.n)
                        last = None
                        for g in range(ncg):
                            last = eng.matmul(prep[:, g * cw:(g + 1) * cw],
                                              sones_r[0:1, :],
                                              sbc2[0:1, SPA + g * cw:
                                                   SPA + (g + 1) * cw],
                                              start=True, stop=True)
                        bb.inc(last)
                        eng.wait_ge(be.sem, b_be + 1)
                        last = None
                        for g in range(ncg):
                            last = eng.matmul(prep[:, g * cw:(g + 1) * cw],
                                              sones_r[0:1, :],
                                              sbc2[0:1, 2 * SPA + g * cw:
                                                   2 * SPA + (g + 1) * cw],
                                              start=True, stop=True)
                        bb.inc(last)
                    emit("tensor", fbT)

                    b_vn = vn.n
                    b_ldn = [ldn[0].n, ldn[1].n]
                    def fsyncN(eng, k=k, col0=col0, b_vn=b_vn, b_ldn=b_ldn,
                               b_xmm=b_xmm, b_sqmm=b_sqmm, b_xsq=b_xsq):
                        for dc in range(c.ndc):
                            s = dc % 2
                            if dc < 4:
                                t = dc + 4 * ((c.ndc - 1 - dc) // 4)
                                eng.wait_ge(xmm.sem, b_xmm + ncg * (t + 1))
                                eng.wait_ge(sqmm.sem, b_sqmm + ncg * (t + 1))
                                eng.wait_ge(xsq.sem, b_xsq + t + 1)
                            else:
                                eng.wait_ge(vn.sem, b_vn + dc - 3)
                            if dc >= 2:
                                eng.wait_ge(ldn[s].sem, b_ldn[s] + 16 * (dc // 2))
                            else:
                                eng.wait_ge(ldn[s].sem, b_ldn[s])
                            ldn[s].inc(eng.dma_start(
                                out=sxn[dc % 4][:],
                                in_=x_fm[dc * PP:(dc + 1) * PP, col0:col0 + SPA]), 16)
                    emit("sync", fsyncN)

                    def fVN(eng, k=k, b_ldn=b_ldn, b_be=b_be):
                        eng.wait_ge(be.sem, b_be + 2)
                        eng.wait_ge(cpuF.sem, cpuF.n)
                        for dc in range(c.ndc):
                            s = dc % 2
                            eng.wait_ge(ldn[s].sem, b_ldn[s] + 16 * (dc // 2 + 1))
                            eng.tensor_tensor(st1[s][:], sxn[dc % 4][:],
                                              srs_b[:], AL.mult)
                            eng.drain()
                            vn.inc(eng.tensor_tensor(
                                sxnbfA[:, dc * SPA:(dc + 1) * SPA], st1[s][:],
                                smrs_b[:], AL.subtract))
                    emit("vector", fVN)

                    b_cwdt = [cwdt[0].n, cwdt[1].n]
                    def fsyncW(eng, k=k, EC0=EC0):
                        for ec in range(c.nhc):
                            s = ec % 2
                            EC = EC0 + ec
                            eng.wait_ge(ccast_dt.sem, 16 * c.nhc)
                            if EC >= 2:
                                eng.wait_ge(cpdt.sem, EC - 1)
                            cwdt[s].inc(eng.dma_start(out=sWdt[s][:], in_=wdtb[ec]), 16)
                    emit("sync", fsyncW)

                    def fTdt(eng, k=k, EC0=EC0, b_cwdt=b_cwdt):
                        for ec in range(c.nhc):
                            s = ec % 2
                            EC = EC0 + ec
                            eng.wait_ge(cwdt[s].sem, b_cwdt[s] + 16 * (ec // 2 + 1))
                            eng.wait_ge(cdtS.sem, EC)
                            if ec == 0:
                                eng.wait_ge(vn.sem, vn.n)
                            last = None
                            for dc in range(c.ndc):
                                for g in range(ncg):
                                    last = eng.matmul(
                                        pdt[:, g * cw:(g + 1) * cw],
                                        sWdt[s][:, dc * PP:(dc + 1) * PP],
                                        sxnbfA[:, dc * SPA + g * cw: dc * SPA + (g + 1) * cw],
                                        start=(dc == 0), stop=(dc == c.ndc - 1))
                            cpdt.inc(last)
                        eng.wait_ge(ld_g.sem, 16)
                        eng.wait_ge(be.sem, be.n)
                        last = None
                        for dc in range(c.ndc):
                            for g in range(ncg):
                                last = eng.matmul(
                                    prep[0:2 * NST, g * cw:(g + 1) * cw],
                                    sWxp[:, dc * 2 * NST:(dc + 1) * 2 * NST],
                                    sxnbfA[:, dc * SPA + g * cw: dc * SPA + (g + 1) * cw],
                                    start=(dc == 0), stop=(dc == c.ndc - 1))
                        cxp.inc(last)
                    emit("tensor", fTdt)

                    def fSdt(eng, k=k, EC0=EC0):
                        for ec in range(c.nhc):
                            EC = EC0 + ec
                            eng.wait_ge(cpdt.sem, EC + 1)
                            eng.activation(stmp[:], pdt[:], ACT.Exp,
                                           bias=sdtb[:, ec:ec + 1])
                            eng.drain()
                            cdtS.inc(eng.activation(sdt[:, ec * SPA:(ec + 1) * SPA],
                                                    stmp[:], ACT.Ln, bias=sK[:, 1:2]))
                            eng.drain()
                    emit("scalar", fSdt)

                    def fVxp(eng, k=k):
                        eng.wait_ge(cxp.sem, k + 1)
                        eng.wait_ge(crow[0].sem, crow[0].n)
                        eng.wait_ge(crow[1].sem, crow[1].n)
                        cxpS.inc(eng.tensor_scalar(sbc2[0:2 * NST, 0:SPA],
                                                   prep[0:2 * NST, :],
                                                   sxpb[:, 0:1], None, AL.add))
                    emit("vector", fVxp)

                    b_ldxu = [ldx[0].n, ldx[1].n]
                    def fsyncU(eng, k=k, col0=col0):
                        for j in range(c.nhc):
                            s = j % 2
                            GJ = k * c.nhc + j
                            if j < 2:
                                eng.wait_ge(cu.sem, cu.n)
                                eng.wait_ge(ldx[s].sem, ldx[s].n)
                            else:
                                eng.wait_ge(cu.sem, GJ - 1)
                            ldx[s].inc(eng.dma_start(
                                out=sxc[s][:],
                                in_=x_h[j * PP:(j + 1) * PP, col0:col0 + SPA]), 16)
                    emit("sync", fsyncU)

                    def fVU(eng, k=k, EC0=EC0, b_ldxu=b_ldxu):
                        for j in range(c.nhc):
                            s = j % 2
                            eng.wait_ge(ldx[s].sem, b_ldxu[s] + 16 * (j // 2 + 1))
                            eng.wait_ge(cdtS.sem, EC0 + j + 1)
                            eng.tensor_tensor(st1[s][:], sxc[s][:],
                                              srs_b[:], AL.mult)
                            eng.drain()
                            eng.tensor_tensor(sxc[s][:], st1[s][:],
                                              smrs_b[:], AL.subtract)
                            eng.drain()
                            cu.inc(eng.tensor_tensor(su[:, j * SPA:(j + 1) * SPA],
                                                     sdt[:, j * SPA:(j + 1) * SPA],
                                                     sxc[s][:], AL.mult))
                    emit("vector", fVU)

                    b_crow = [crow[0].n, crow[1].n]
                    b_cwgF = [cwgF[0].n, cwgF[1].n]
                    def fsyncR(eng, k=k, GN0=GN0):
                        eng.wait_ge(cxpS.sem, k + 1)
                        for n in range(NST):
                            s = n % 2
                            GN = GN0 + n
                            if GN >= 2:
                                eng.wait_ge(cbb.sem, GN - 1)
                            p0 = 32 + 32 * s
                            crow[s].inc(eng.dma_start(
                                out=sbc2[p0:p0 + 1, 0:SPA],
                                in_=sbc2[n:n + 1, 0:SPA]), 16)
                            crow[s].inc(eng.dma_start(
                                out=sbc2[p0:p0 + 1, SPA:2 * SPA],
                                in_=sbc2[NST + n:NST + n + 1, 0:SPA]), 16)
                    emit("sync", fsyncR)

                    b_cwuF = [cwuF[0].n, cwuF[1].n]
                    def fTscan(eng, k=k, GN0=GN0, GIC0=GIC0,
                               b_crow=b_crow, b_cwuF=b_cwuF, b_cwgF=b_cwgF):
                        for n in range(NST):
                            s = n % 2
                            GN = GN0 + n
                            eng.wait_ge(crow[s].sem, b_crow[s] + 32 * (n // 2 + 1))
                            if GN >= 1:
                                eng.wait_ge(cbe.sem, GN)
                            p0 = 32 + 32 * s
                            for g in range(ncg):
                                eng.matmul(pB[:, g * cw:(g + 1) * cw],
                                           sones_r[p0:p0 + 1, :],
                                           sbc2[p0:p0 + 1, g * cw:(g + 1) * cw],
                                           start=True, stop=True)
                            last = None
                            for g in range(ncg):
                                last = eng.matmul(
                                    pC[:, g * cw:(g + 1) * cw],
                                    sones_r[p0:p0 + 1, :],
                                    sbc2[p0:p0 + 1, SPA + g * cw:SPA + (g + 1) * cw],
                                    start=True, stop=True)
                            cbb.inc(last)
                            for ic in (sil_at.get(n, []) if k >= 1 else []):
                                GIC = GIC0 + ic
                                if ic < NR:
                                    eng.wait_ge(cgur.sem, 16 * NR)
                                    gsrc = lambda dc, ic=ic: sWgR[
                                        :, (ic * c.ndc + dc) * PP:
                                        (ic * c.ndc + dc + 1) * PP]
                                else:
                                    si = ic - NR
                                    eng.wait_ge(cwgF[si % 2].sem,
                                                b_cwgF[si % 2] + 16 * (si // 2 + 1))
                                    gsrc = lambda dc, ic=ic: sWgFs[
                                        (ic - NR) % 2][:, dc * PP:(dc + 1) * PP]
                                eng.wait_ge(csgF.sem, GIC)
                                last = None
                                for dc in range(c.ndc):
                                    last = eng.matmul(
                                        pgF[:], gsrc(dc),
                                        sxnbfA[:, dc * SPA:(dc + 1) * SPA],
                                        start=(dc == 0), stop=(dc == c.ndc - 1))
                                cpgF.inc(last)
                                eng.wait_ge(cwuF[ic % 2].sem,
                                            b_cwuF[ic % 2] + 16 * (ic // 2 + 1))
                                eng.wait_ge(cuev.sem, GIC)
                                last = None
                                for dc in range(c.ndc):
                                    last = eng.matmul(
                                        puF[:],
                                        sWuF[ic % 2][:, dc * PP:(dc + 1) * PP],
                                        sxnbfA[:, dc * SPA:(dc + 1) * SPA],
                                        start=(dc == 0), stop=(dc == c.ndc - 1))
                                cpuF.inc(last)
                    emit("tensor", fTscan)

                    b_swu = [cwuF[0].n, cwuF[1].n]
                    b_swg = [cwgF[0].n, cwgF[1].n]
                    odm = od_odm[k]
                    def fSscan(eng, k=k, GN0=GN0, GI0=GI0, GIC0=GIC0,
                               b_swu=b_swu, b_swg=b_swg, odm=odm):
                        for ic in (range(min(2, c.nih)) if k >= 1 else []):
                            eng.wait_ge(ccast_u.sem, 16 * c.nih)
                            eng.wait_ge(cwuF[ic % 2].sem, b_swu[ic % 2])
                            cwuF[ic % 2].inc(eng.dma_start(out=sWuF[ic % 2][:],
                                                           in_=wub[ic]), 16)
                        for ic in (range(NR, min(NR + 2, c.nih))
                                   if k >= 1 else []):
                            si = ic - NR
                            eng.wait_ge(ccast_g.sem, 16 * (c.nih - NR))
                            eng.wait_ge(cwgF[si % 2].sem, b_swg[si % 2])
                            cwgF[si % 2].inc(eng.dma_start(
                                out=sWgFs[si % 2][:], in_=wgb[si]), 16)
                        for n in range(NST):
                            s = n % 2
                            GN = GN0 + n
                            eng.wait_ge(cbb.sem, GN + 1)
                            if GN >= 2:
                                eng.wait_ge(cadd.sem, GN - 1)
                            eng.activation(sBb[s][:], pB[:], ACT.Copy)
                            cbe.inc(eng.activation(sCb[s][:], pC[:], ACT.Copy))
                            for j in range(c.nhc):
                                GI = GI0 + n * c.nhc + j
                                sl = GI % 4
                                col = j * NST + n
                                if GI >= 4:
                                    eng.wait_ge(cdAf.sem, GI - 3)
                                cdA.inc(eng.activation(
                                    sdA[sl][:], sdt[:, j * SPA:(j + 1) * SPA],
                                    ACT.Exp, scale=sA[:, col:col + 1]))
                            ics = sil_at.get(n, []) if k >= 1 else []
                            if ics:
                                load_act(eng, 18)
                                for ic in ics:
                                    GIC = GIC0 + ic
                                    eng.wait_ge(cpgF.sem, GIC + 1)
                                    if GIC >= 4:
                                        eng.wait_ge(chdnF.sem, GIC - 3)
                                    csgF.inc(eng.activation(
                                        ssgF[GIC % 4][:], pgF[:], ACT.Silu,
                                        scale=sgs[:, ic:ic + 1]))
                                load_act(eng, 6)
                                for ic in ics:
                                    GIC = GIC0 + ic
                                    eng.wait_ge(cpuF.sem, GIC + 1)
                                    if GIC >= 4:
                                        eng.wait_ge(chdnF.sem, GIC - 3)
                                    cuev.inc(eng.activation(
                                        suev[GIC % 4][:], puF[:], ACT.Copy,
                                        scale=sus[:, ic:ic + 1]))
                                    if ic + 2 < c.nih:
                                        eng.wait_ge(ccast_u.sem, 16 * c.nih)
                                        eng.wait_ge(cwuF[ic % 2].sem,
                                                    b_swu[ic % 2] +
                                                    16 * (ic // 2 + 1))
                                        cwuF[ic % 2].inc(eng.dma_start(
                                            out=sWuF[ic % 2][:],
                                            in_=wub[ic + 2]), 16)
                                    if ic >= NR and ic + 2 < c.nih:
                                        si = ic + 2 - NR
                                        eng.wait_ge(ccast_g.sem,
                                                    16 * (c.nih - NR))
                                        eng.wait_ge(cwgF[si % 2].sem,
                                                    b_swg[si % 2] +
                                                    16 * (si // 2))
                                        cwgF[si % 2].inc(eng.dma_start(
                                            out=sWgFs[si % 2][:],
                                            in_=wgb[si]), 16)
                            for p in odm.get(n, []):
                                sp, dp, lo, hi = od_pieces[p]
                                eng.wait_ge(codl[p % 2].sem, 16 * (p // 2 + 1))
                                if p >= 2:
                                    eng.wait_ge(cods[p % 2].sem, 16 * (p // 2))
                                codc.inc(eng.activation(
                                    sob[p % 2][:, 0:hi - lo],
                                    sow[p % 2][:, 0:hi - lo], ACT.Copy))
                                eng.wait_ge(codc.sem, p + 1)
                                cods[p % 2].inc(eng.dma_start(
                                    out=dp[:, lo:hi],
                                    in_=sob[p % 2][:, 0:hi - lo]), 16)
                    emit("scalar", fSscan)

                    def fVscan(eng, k=k, GN0=GN0, GI0=GI0):
                        a_su = su.ap().rearrange("p (nj t) -> p nj t", nj=c.nhc)
                        a_sh = shCr.ap().rearrange("p (nj t) -> p nj t",
                                                   nj=c.nhc)
                        for n in range(NST):
                            GN = GN0 + n
                            eng.wait_ge(cbe.sem, GN + 1)
                            if n == 0:
                                eng.wait_ge(cyd.sem, 16 * c.nhc * k)
                                eng.wait_ge(cu.sem, (k + 1) * c.nhc)
                            if GN >= 1:
                                eng.wait_ge(cadd.sem, GN)
                            b3 = bass.AP(sBb[GN % 2].ap().tensor, 0,
                                         [[SPA, PP], [0, c.nhc], [1, SPA]])
                            cdbx.inc(eng.tensor_tensor(a_sh, a_su, b3,
                                                       AL.mult))
                            for j in range(c.nhc):
                                GI = GI0 + n * c.nhc + j
                                col = j * NST + n
                                eng.wait_ge(cdA.sem, GI + 1)
                                if j == 0:
                                    eng.wait_ge(cdbx.sem, GN + 1)
                                cdAf.inc(eng.tensor_tensor_scan(
                                    shCr[:, j * SPA:(j + 1) * SPA],
                                    sdA[GI % 4][:],
                                    shCr[:, j * SPA:(j + 1) * SPA],
                                    sH[:, col:col + 1], AL.mult, AL.add))
                            eng.wait_ge(cdAf.sem, GI0 + (n + 1) * c.nhc)
                            eng.tensor_copy(sH[:, n::NST],
                                            shCr[:, SPA - 1::SPA])
                            eng.drain()
                            c3 = bass.AP(sCb[GN % 2].ap().tensor, 0,
                                         [[SPA, PP], [0, c.nhc], [1, SPA]])
                            eng.tensor_tensor(a_sh, a_sh, c3, AL.mult)
                            eng.drain()
                            if n == 0:
                                cadd.inc(eng.tensor_copy(sy[:], shCr[:]))
                            else:
                                cadd.inc(eng.tensor_tensor(sy[:], sy[:],
                                                           shCr[:], AL.add))
                    emit("vector", fVscan)

                    def fgpScan(eng, k=k, GN0=GN0, GI0=GI0, GIC0=GIC0):
                        for n in range(NST):
                            GN = GN0 + n
                            if n == 2 and k >= 2:
                                eng.wait_ge(cyd.sem, tot_cyd[k - 2])
                                eng.wait_ge(chd[0].sem, tot_chd0[k - 2])
                                eng.wait_ge(chd[1].sem, tot_chd1[k - 2])
                                cag.inc(eng.collective_compute(
                                    "AllGather", AL.bypass, replica_groups=groups,
                                    ins=[cc_in[k - 2]], outs=[cc_out[k - 2]]))
                                cag.inc(eng.collective_compute(
                                    "AllGather", AL.bypass, replica_groups=groups,
                                    ins=[cc_hin[k - 2]], outs=[cc_hout[k - 2]]))
                            for ic in (sil_gp.get(n, []) if k >= 1 else []):
                                GIC = GIC0 + ic
                                eng.wait_ge(cuev.sem, GIC + 1)
                                eng.wait_ge(csgF.sem, GIC + 1)
                                if GIC >= 2:
                                    eng.wait_ge(chd[GIC % 2].sem, 16 * (GIC // 2))
                                chdnF.inc(eng.tensor_tensor(
                                    shdnF[GIC % 2][:], suev[GIC % 4][:],
                                    ssgF[GIC % 4][:], AL.mult))
                                eng.wait_ge(chdnF.sem, GIC + 1)
                                chd[GIC % 2].inc(eng.dma_start(
                                    out=cc_hin[k][ic * PP:(ic + 1) * PP, :],
                                    in_=shdnF[GIC % 2][:]), 16)
                            for p in od_issue.get(GN, []):
                                sp, dp, lo, hi = od_pieces[p]
                                if p >= 2:
                                    eng.wait_ge(codc.sem, p - 1)
                                codl[p % 2].inc(eng.dma_start(
                                    out=sow[p % 2][:, 0:hi - lo],
                                    in_=sp[:, lo:hi]), 16)
                        eng.wait_ge(cadd.sem, (k + 1) * NST)
                        for j in range(c.nhc):
                            cyd.inc(eng.dma_start(
                                out=cc_in[k][j * PP:(j + 1) * PP, :],
                                in_=sy[:, j * SPA:(j + 1) * SPA]), 16)
                        for m in range(NST, NST + 3):
                            for ic in (sil_gp.get(m, []) if k >= 1 else []):
                                GIC = GIC0 + ic
                                eng.wait_ge(cuev.sem, GIC + 1)
                                eng.wait_ge(csgF.sem, GIC + 1)
                                if GIC >= 2:
                                    eng.wait_ge(chd[GIC % 2].sem, 16 * (GIC // 2))
                                chdnF.inc(eng.tensor_tensor(
                                    shdnF[GIC % 2][:], suev[GIC % 4][:],
                                    ssgF[GIC % 4][:], AL.mult))
                                eng.wait_ge(chdnF.sem, GIC + 1)
                                chd[GIC % 2].inc(eng.dma_start(
                                    out=cc_hin[k][ic * PP:(ic + 1) * PP, :],
                                    in_=shdnF[GIC % 2][:]), 16)
                    emit("gpsimd", fgpScan)

                    if k == 0:
                        def fT0(eng):
                            eng.wait_ge(cgur.sem, 16 * NR)
                            for ic in range(c.nih):
                                if ic < NR:
                                    gsrc = lambda dc, ic=ic: sWgR[
                                        :, (ic * c.ndc + dc) * PP:
                                        (ic * c.ndc + dc + 1) * PP]
                                else:
                                    si = ic - NR
                                    eng.wait_ge(cwgF[si % 2].sem,
                                                16 * (si // 2 + 1))
                                    gsrc = lambda dc, ic=ic: sWgFs[
                                        (ic - NR) % 2][:, dc * PP:(dc + 1) * PP]
                                eng.wait_ge(csgF.sem, ic)
                                last = None
                                for dc in range(c.ndc):
                                    last = eng.matmul(
                                        pgF[:], gsrc(dc),
                                        sxnbfA[:, dc * SPA:(dc + 1) * SPA],
                                        start=(dc == 0), stop=(dc == c.ndc - 1))
                                cpgF.inc(last)
                                eng.wait_ge(cwuF[ic % 2].sem, 16 * (ic // 2 + 1))
                                eng.wait_ge(cuev.sem, ic)
                                last = None
                                for dc in range(c.ndc):
                                    last = eng.matmul(
                                        puF[:],
                                        sWuF[ic % 2][:, dc * PP:(dc + 1) * PP],
                                        sxnbfA[:, dc * SPA:(dc + 1) * SPA],
                                        start=(dc == 0), stop=(dc == c.ndc - 1))
                                cpuF.inc(last)
                        emit("tensor", fT0)

                        def fS0(eng):
                            for ic in range(min(2, c.nih)):
                                eng.wait_ge(ccast_u.sem, 16 * c.nih)
                                cwuF[ic % 2].inc(eng.dma_start(
                                    out=sWuF[ic % 2][:], in_=wub[ic]), 16)
                            for ic in range(NR, min(NR + 2, c.nih)):
                                si = ic - NR
                                eng.wait_ge(ccast_g.sem, 16 * (c.nih - NR))
                                cwgF[si % 2].inc(eng.dma_start(
                                    out=sWgFs[si % 2][:], in_=wgb[si]), 16)
                            for ic in range(c.nih):
                                eng.wait_ge(cpgF.sem, ic + 1)
                                if ic >= 4:
                                    eng.wait_ge(chdnF.sem, ic - 3)
                                load_act(eng, 18)
                                csgF.inc(eng.activation(
                                    ssgF[ic % 4][:], pgF[:], ACT.Silu,
                                    scale=sgs[:, ic:ic + 1]))
                                load_act(eng, 6)
                                eng.wait_ge(cpuF.sem, ic + 1)
                                cuev.inc(eng.activation(
                                    suev[ic % 4][:], puF[:], ACT.Copy,
                                    scale=sus[:, ic:ic + 1]))
                                if ic + 2 < c.nih:
                                    eng.wait_ge(cwuF[ic % 2].sem,
                                                16 * (ic // 2 + 1))
                                    cwuF[ic % 2].inc(eng.dma_start(
                                        out=sWuF[ic % 2][:], in_=wub[ic + 2]), 16)
                                if ic >= NR and ic + 2 < c.nih:
                                    si = ic + 2 - NR
                                    eng.wait_ge(cwgF[si % 2].sem,
                                                16 * (si // 2))
                                    eng.wait_ge(cpgF.sem, ic + 1)
                                    cwgF[si % 2].inc(eng.dma_start(
                                        out=sWgFs[si % 2][:], in_=wgb[si]), 16)
                        emit("scalar", fS0)

                        def fgp0(eng):
                            for ic in range(c.nih):
                                eng.wait_ge(cuev.sem, ic + 1)
                                eng.wait_ge(csgF.sem, ic + 1)
                                if ic >= 2:
                                    eng.wait_ge(chd[ic % 2].sem, 16 * (ic // 2))
                                chdnF.inc(eng.tensor_tensor(
                                    shdnF[ic % 2][:], suev[ic % 4][:],
                                    ssgF[ic % 4][:], AL.mult))
                                eng.wait_ge(chdnF.sem, ic + 1)
                                chd[ic % 2].inc(eng.dma_start(
                                    out=cc_hin[0][ic * PP:(ic + 1) * PP, :],
                                    in_=shdnF[ic % 2][:]), 16)
                        emit("gpsimd", fgp0)

                    tot_cyd.append(cyd.n)
                    tot_chd0.append(chd[0].n)
                    tot_chd1.append(chd[1].n)

                def fgpAGl(eng):
                    eng.wait_ge(cyd.sem, cyd.n)
                    eng.wait_ge(chd[0].sem, chd[0].n)
                    eng.wait_ge(chd[1].sem, chd[1].n)
                    for k2 in range(max(0, c.NSP - 2), c.NSP):
                        cag.inc(eng.collective_compute(
                            "AllGather", AL.bypass, replica_groups=groups,
                            ins=[cc_in[k2]], outs=[cc_out[k2]]))
                        cag.inc(eng.collective_compute(
                            "AllGather", AL.bypass, replica_groups=groups,
                            ins=[cc_hin[k2]], outs=[cc_hout[k2]]))
                emit("gpsimd", fgpAGl)
                barrier()

        cw, ncg = c.cwB, c.ncgB
        for sidx in range(c.nown):
            colB = sidx * SPB
            with contextlib.ExitStack() as esB:
                sbB = lambda name, shape, dt: esB.enter_context(
                    nc.sbuf_tensor(name, shape, dt))
                syfull = sbB(f"syfull{sidx}", [PP, c.ndc * SPB], BF16)
                shdn = sbB(f"shdn{sidx}", [PP, c.nic * SPB], BF16)
                scca = [sbB(f"scca{sidx}_{i}", [PP, SPA], BF16) for i in range(2)]
                sccb = [sbB(f"sccb{sidx}_{i}", [PP, SPA], BF16) for i in range(2)]
                cld = [ch("cld0"), ch("cld1")]
                cyf = ch("cyf")
                b_cld = [cld[0].n, cld[1].n]
                b_cyf = cyf.n
                items = []
                for g in range(c.GPS):
                    ka = 2 * (sidx * c.GPS + g)
                    for dc in range(c.ndc):
                        items.append((cc_out[ka], cc_out[ka + 1], dc,
                                      syfull, dc * SPB + g * SPA))
                    for ic in range(c.nic):
                        items.append((cc_hout[ka], cc_hout[ka + 1], ic,
                                      shdn, ic * SPB + g * SPA))

                def fsyncB1(eng, items=items):
                    eng.wait_ge(cag.sem, cag.n)
                    for t, (ta, tb, chunk, dst, off) in enumerate(items):
                        s = t % 2
                        if t >= 2:
                            eng.wait_ge(cyf.sem, b_cyf + t - 1)
                        cld[s].inc(eng.dma_start(
                            out=scca[s][:],
                            in_=ta[chunk * PP:(chunk + 1) * PP, :]), 16)
                        cld[s].inc(eng.dma_start(
                            out=sccb[s][:],
                            in_=tb[chunk * PP:(chunk + 1) * PP, :]), 16)
                emit("sync", fsyncB1)

                def fVB1(eng, items=items):
                    for t, (ta, tb, chunk, dst, off) in enumerate(items):
                        s = t % 2
                        eng.wait_ge(cld[s].sem, b_cld[s] + 32 * (t // 2 + 1))
                        eng.tensor_scalar(scca[s][:], scca[s][:], smks[:, 0:1],
                                          None, AL.mult)
                        eng.tensor_scalar(sccb[s][:], sccb[s][:], smks[:, 1:2],
                                          None, AL.mult)
                        eng.drain()
                        cyf.inc(eng.tensor_tensor(dst[:, off:off + SPA],
                                                  scca[s][:], sccb[s][:], AL.add))
                emit("vector", fVB1)
                barrier()

                sWo = [sbB(f"sWo{sidx}_{i}", [PP, c.ndc * PP], BF16) for i in range(2)]
                sWdn = [sbB(f"sWdn{sidx}_{i}", [PP, c.nic * PP], BF16) for i in range(2)]
                sxres = [sbB(f"sxres{sidx}_{i}", [PP, SPB], F32) for i in range(2)]
                so1 = [sbB(f"so1{sidx}_{i}", [PP, SPB], F32) for i in range(2)]
                sout_t = [sbB(f"sout{sidx}_{i}", [PP, SPB], F32) for i in range(2)]
                with contextlib.ExitStack() as ps:
                    po = [ps.enter_context(nc.psum_tensor(
                        f"po{sidx}_{i}", [PP, SPB], F32)) for i in range(2)]
                    pd = [ps.enter_context(nc.psum_tensor(
                        f"pd{sidx}_{i}", [PP, SPB], F32)) for i in range(2)]
                    cwq = [ch("cwq0"), ch("cwq1"), ch("cwq2")]
                    cxr = [ch("cxr0"), ch("cxr1")]
                    cpo, cpd = ch("cpo"), ch("cpd")
                    co1, cout = ch("co1"), ch("cout")
                    sto = [ch("sto0"), ch("sto1")]
                    b_cwq = [cwq[0].n, cwq[1].n, cwq[2].n]
                    b_cxr = [cxr[0].n, cxr[1].n]
                    b_cpo, b_cpd = cpo.n, cpd.n
                    b_co1, b_cout = co1.n, cout.n
                    b_sto = [sto[0].n, sto[1].n]
                    qof, qcnt = {}, [0, 0, 0]
                    for mc in range(c.nmc):
                        q = mc % 3
                        qcnt[q] += 32
                        qof[mc] = (q, qcnt[q])

                    def wload1(eng, q, mc):
                        eng.wait_ge(cods[0].sem, cods[0].n)
                        eng.wait_ge(cods[1].sem, cods[1].n)
                        if mc >= 2:
                            eng.wait_ge(cpd.sem, b_cpd + mc - 1)
                        cwq[q].inc(eng.dma_start(out=sWo[mc % 2][:],
                                                 in_=woutb[mc]), 16)
                        cwq[q].inc(eng.dma_start(out=sWdn[mc % 2][:],
                                                 in_=wdb[mc]), 16)

                    def fsyncB3(eng):
                        for mc in range(c.nmc):
                            s = mc % 2
                            if mc >= 2:
                                eng.wait_ge(co1.sem, b_co1 + mc - 1)
                            cxr[s].inc(eng.dma_start(
                                out=sxres[s][:],
                                in_=xB[mc * PP:(mc + 1) * PP, colB:colB + SPB]), 16)
                            if mc % 3 == 0:
                                wload1(eng, 0, mc)
                    emit("sync", fsyncB3)

                    def factB3(eng):
                        for mc in range(c.nmc):
                            if mc >= 2:
                                eng.wait_ge(cout.sem, b_cout + mc - 1)
                                sto[mc % 2].inc(eng.dma_start(
                                    out=out[(mc - 2) * PP:(mc - 1) * PP,
                                            colB:colB + SPB],
                                    in_=sout_t[mc % 2][:]), 16)
                            if mc % 3 == 1:
                                wload1(eng, 1, mc)
                        for mc in range(c.nmc - 2, c.nmc):
                            eng.wait_ge(cout.sem, b_cout + mc + 1)
                            sto[mc % 2].inc(eng.dma_start(
                                out=out[mc * PP:(mc + 1) * PP, colB:colB + SPB],
                                in_=sout_t[mc % 2][:]), 16)
                        eng.wait_ge(sto[0].sem, sto[0].n)
                        eng.wait_ge(sto[1].sem, sto[1].n)
                    emit("scalar", factB3)

                    def fgpB3(eng):
                        for mc in range(c.nmc):
                            if mc % 3 == 2:
                                wload1(eng, 2, mc)
                    emit("gpsimd", fgpB3)

                    def fTB3(eng):
                        for mc in range(c.nmc):
                            s = mc % 2
                            q, cnt = qof[mc]
                            eng.wait_ge(cwq[q].sem, b_cwq[q] + cnt)
                            if mc >= 2:
                                eng.wait_ge(cout.sem, b_cout + mc - 1)
                                eng.wait_ge(co1.sem, b_co1 + mc - 1)
                            last = None
                            for dc in range(c.ndc):
                                for g in range(ncg):
                                    last = eng.matmul(
                                        po[s][:, g * cw:(g + 1) * cw],
                                        sWo[s][:, dc * PP:(dc + 1) * PP],
                                        syfull[:, dc * SPB + g * cw: dc * SPB + (g + 1) * cw],
                                        start=(dc == 0), stop=(dc == c.ndc - 1))
                            cpo.inc(last)
                            last = None
                            for ic in range(c.nic):
                                for g in range(ncg):
                                    last = eng.matmul(
                                        pd[s][:, g * cw:(g + 1) * cw],
                                        sWdn[s][:, ic * PP:(ic + 1) * PP],
                                        shdn[:, ic * SPB + g * cw: ic * SPB + (g + 1) * cw],
                                        start=(ic == 0), stop=(ic == c.nic - 1))
                            cpd.inc(last)
                    emit("tensor", fTB3)

                    def fVB3(eng):
                        for mc in range(c.nmc):
                            s = mc % 2
                            eng.wait_ge(cpd.sem, b_cpd + mc + 1)
                            eng.wait_ge(cxr[s].sem, b_cxr[s] + 16 * (mc // 2 + 1))
                            eng.wait_ge(cpo.sem, b_cpo + mc + 1)
                            if mc >= 2:
                                eng.wait_ge(sto[s].sem, b_sto[s] + 16 * ((mc - 2) // 2 + 1))
                            co1.inc(eng.scalar_tensor_tensor(
                                so1[s][:], pd[s][:], sds[:, mc:mc + 1], sxres[s][:],
                                AL.mult, AL.add))
                            eng.drain()
                            cout.inc(eng.scalar_tensor_tensor(
                                sout_t[s][:], po[s][:], soutb[:, mc:mc + 1], so1[s][:],
                                AL.add, AL.add))
                    emit("vector", fVB3)
                    barrier()

        sto0, sto1 = ch("sto0"), ch("sto1")
        def ffinal(eng):
            eng.wait_ge(sto0.sem, sto0.n)
            eng.wait_ge(sto1.sem, sto1.n)
        emit("sync", ffinal)
    return nc



def pack_inputs(cfg: Cfg, core_id, x, A_log, x_proj_w, x_proj_b, dt_proj_w, dt_proj_b,
                out_proj_w, out_proj_b, gate_w, up_w, down_w, gate_s, up_s, down_s,
                shared_cache=None):
    c = cfg
    seq, half = core_id // 2, core_id % 2
    xb = np.ascontiguousarray(x[seq].T)
    dlo, dhi = half * c.DH, (half + 1) * c.DH
    own_blocks = [2 * i + half for i in range(c.NSP // 2)]
    xBv = np.ascontiguousarray(
        np.concatenate([xb[:, k * c.SPA:(k + 1) * c.SPA] for k in own_blocks], axis=1))

    def blob(wT):
        K, M = wT.shape
        a = wT.reshape(K // PP, PP, M // PP, PP)
        return np.ascontiguousarray(
            a.transpose(2, 1, 0, 3).reshape(M // PP, PP, (K // PP) * PP))

    sc = shared_cache if shared_cache is not None else {}
    if "wdt0" not in sc:
        for h in (0, 1):
            sc[f"wdt{h}"] = np.ascontiguousarray(
                blob(dt_proj_w.T[:, h * c.DH:(h + 1) * c.DH]), dtype=np.float32)
        a = x_proj_w.T.reshape(c.ndc, PP, 2 * c.NST)
        sc["wxp"] = np.ascontiguousarray(
            a.transpose(1, 0, 2).reshape(PP, c.ndc * 2 * c.NST), dtype=np.float32)
        sc["wout"] = np.ascontiguousarray(blob(out_proj_w.T), dtype=np.float32)
        gb, ub = blob(gate_w.T), blob(up_w.T)
        for h in (0, 1):
            sc[f"wg{h}"] = np.ascontiguousarray(
                gb[h * c.nih:(h + 1) * c.nih], dtype=np.float32)
            sc[f"wu{h}"] = np.ascontiguousarray(
                ub[h * c.nih:(h + 1) * c.nih], dtype=np.float32)
        sc["wd"] = np.ascontiguousarray(blob(down_w.T), dtype=np.float32)
    wdtv = sc[f"wdt{half}"]
    wxpv, woutv, wdv = sc["wxp"], sc["wout"], sc["wd"]
    wgv, wuv = sc[f"wg{half}"], sc[f"wu{half}"]

    alogv = np.ascontiguousarray(
        A_log[dlo:dhi].reshape(c.nhc, PP, c.NST).transpose(1, 0, 2).reshape(
            PP, c.nhc * c.NST))
    dtbv = np.ascontiguousarray(dt_proj_b[dlo:dhi].reshape(c.nhc, PP).T)
    xpbv = x_proj_b.reshape(2 * c.NST, 1)
    outbv = np.ascontiguousarray(out_proj_b.reshape(c.nmc, PP).T)
    dscv = np.ascontiguousarray(down_s.reshape(c.nmc, PP).T)
    ilo, ihi = half * c.nih, (half + 1) * c.nih
    gscv = np.ascontiguousarray(gate_s.reshape(c.nic, PP)[ilo:ihi].T)
    uscv = np.ascontiguousarray(up_s.reshape(c.nic, PP)[ilo:ihi].T)
    mksv = np.zeros((PP, 2), np.float32)
    mksv[:, 0] = 1.0 if half == 0 else 0.0
    mksv[:, 1] = 0.0 if half == 0 else 1.0

    f32 = lambda a: np.ascontiguousarray(a, dtype=np.float32)
    return {
        "x_fm": f32(xb), "x_h": f32(xb[dlo:dhi]), "xB": f32(xBv),
        "wdt": wdtv, "wxp": wxpv, "wout": woutv, "wg": wgv, "wu": wuv, "wd": wdv,
        "alog": f32(alogv), "dtb": f32(dtbv), "xpb": f32(xpbv), "outb": f32(outbv),
        "dsc": f32(dscv), "gsc": f32(gscv), "usc": f32(uscv), "mks": f32(mksv),
    }


def assemble_output(cfg: Cfg, results):
    c = cfg
    B = len(results) // 2
    full = np.empty((B, c.L, c.D), np.float32)
    for core_id, r in enumerate(results):
        seq, half = core_id // 2, core_id % 2
        o = np.asarray(r["out"])
        own_blocks = [2 * i + half for i in range(c.NSP // 2)]
        for bi, k in enumerate(own_blocks):
            full[seq, k * c.SPA:(k + 1) * c.SPA, :] = \
                o[:, bi * c.SPA:(bi + 1) * c.SPA].T
    return full



_CACHE = {}


def _get_nc(cfg):
    key = (cfg.D, cfg.L, cfg.INTER, cfg.SPA, cfg.SPB, cfg.n_cores)
    if key not in _CACHE:
        _CACHE[key] = build(cfg)
    return _CACHE[key]


def kernel(L0_attn_x, L0_attn_A_log, L0_attn_x_proj_w, L0_attn_x_proj_b,
           L0_attn_dt_proj_w, L0_attn_dt_proj_b, L0_attn_out_proj_w,
           L0_attn_out_proj_b, L0_ffn_gate_w, L0_ffn_up_w, L0_ffn_down_w,
           L0_ffn_gate_s, L0_ffn_up_s, L0_ffn_down_s):
    from concourse.bass_utils import run_bass_kernel_spmd

    cfg = Cfg(D=2048, L=4096, INTER=5632, SPA=512, SPB=1024, NST=16, n_cores=8)
    args = (np.asarray(L0_attn_x, np.float32),
            np.asarray(L0_attn_A_log, np.float32),
            np.asarray(L0_attn_x_proj_w, np.float32),
            np.asarray(L0_attn_x_proj_b, np.float32),
            np.asarray(L0_attn_dt_proj_w, np.float32),
            np.asarray(L0_attn_dt_proj_b, np.float32),
            np.asarray(L0_attn_out_proj_w, np.float32),
            np.asarray(L0_attn_out_proj_b, np.float32),
            np.asarray(L0_ffn_gate_w, np.float32),
            np.asarray(L0_ffn_up_w, np.float32),
            np.asarray(L0_ffn_down_w, np.float32),
            np.asarray(L0_ffn_gate_s, np.float32),
            np.asarray(L0_ffn_up_s, np.float32),
            np.asarray(L0_ffn_down_s, np.float32))

    shared = {}
    in_maps = [pack_inputs(cfg, cid, *args, shared_cache=shared)
               for cid in range(cfg.n_cores)]
    nc = _get_nc(cfg)
    res = run_bass_kernel_spmd(nc, in_maps, core_ids=list(range(cfg.n_cores)))
    return assemble_output(cfg, res.results)
